# version 28
# baseline (speedup 1.0000x reference)
import numpy as np
import ml_dtypes
import concourse.bass as bass
import concourse.mybir as mybir
from concourse.bass_utils import run_bass_kernel_spmd
from concourse.bass import IndirectOffsetOnAxis

F32 = mybir.dt.float32
BF16 = mybir.dt.bfloat16
AF = mybir.ActivationFunctionType
ALU = mybir.AluOpType
AX = mybir.AxisListType

B, S, D, H, HD, I, E = 2, 8192, 1024, 16, 64, 2048, 8
NC_N = 8
TC = 2048
NT = 16
NSEC = 8
PAIRS = 8
NKC = 64
ATT_SCALE = 1.0 / 32.0
CSHIFT = 10.0
LN_E1 = 1e-5
COMB_A = 1.0 + 1e-6
COMB_B = 1e-11
CAP = 512


class Pl:

    def __init__(self, nc):
        self.nc = nc
        names = ["pe", "act", "dve", "dm_const", "dm_w",
                 "dm_x0", "dm_x1", "dm_k0", "dm_k1", "dm_v",
                 "dm_kv0", "dm_kv1", "dm_x1w", "dm_x1w1", "dm_x1r0", "dm_x1r1",
                 "dm_wg0", "dm_wg1", "dm_wg2", "dm_wg3", "dm_wd", "dm_z",
                 "dm_sp0", "dm_sp1", "dm_xe", "dm_ye", "dm_yg0", "dm_yg1",
                 "dm_yg2", "dm_yg3", "out"]
        self.sem = {n: nc.alloc_semaphore("sem_" + n) for n in names}
        self.c = {n: 0 for n in names}
        self.q = {"pe": [], "act": [], "dve": [], "gpsimd": [], "sync": []}

    def w(self, eng, sem, val):
        if val is None or val <= 0:
            return
        s = self.sem[sem]
        self.q[eng].append(lambda h, s=s, v=val: h.wait_ge(s, v))

    def do(self, eng, fn, sem, k=1):
        self.c[sem] += k
        s, kk = self.sem[sem], k
        self.q[eng].append(lambda h, fn=fn, s=s, kk=kk: fn(h).then_inc(s, kk))
        return self.c[sem]

    def do0(self, eng, fn):
        self.q[eng].append(fn)

    def snap(self):
        return dict(self.c)

    def barrier(self, engs=("pe", "act", "dve", "gpsimd", "sync")):
        sn = self.snap()
        for e in engs:
            for s in self.sem:
                self.w(e, s, sn[s])
        return sn


def build_program(upto=99, experts=None, p3_only=False):
    nc = bass.Bass("TRN2", target_bir_lowering=False)

    x_d = nc.dram_tensor("x", [S, D], F32, kind="ExternalInput")
    lns_d = nc.dram_tensor("ln_scale", [D], F32, kind="ExternalInput")
    lnb_d = nc.dram_tensor("ln_bias", [D], F32, kind="ExternalInput")
    wq_d = nc.dram_tensor("wq", [D, D], F32, kind="ExternalInput")
    wk_d = nc.dram_tensor("wk", [D, D], F32, kind="ExternalInput")
    wv_d = nc.dram_tensor("wv", [D, D], F32, kind="ExternalInput")
    gw_d = nc.dram_tensor("gate_w", [D, E], F32, kind="ExternalInput")
    shg_d = nc.dram_tensor("sh_gate", [D, I], F32, kind="ExternalInput")
    shu_d = nc.dram_tensor("sh_up", [D, I], F32, kind="ExternalInput")
    shd_d = nc.dram_tensor("sh_down", [I, D], F32, kind="ExternalInput")
    rtg_d = nc.dram_tensor("rt_gate", [E, D, I], F32, kind="ExternalInput")
    rtu_d = nc.dram_tensor("rt_up", [E, D, I], F32, kind="ExternalInput")
    rtd_d = nc.dram_tensor("rt_down", [E, I, D], F32, kind="ExternalInput")
    idb_d = nc.dram_tensor("ident_bf", [128, 128], BF16, kind="ExternalInput")
    iota_d = nc.dram_tensor("iota128", [128, 1], F32, kind="ExternalInput")
    ebase_d = nc.dram_tensor("ebase", [E], F32, kind="ExternalInput")
    idf_d = nc.dram_tensor("ident_f32", [128, 128], F32, kind="ExternalInput")

    out_d = nc.dram_tensor("out", [TC, D], F32, kind="ExternalOutput")

    k_dram = nc.dram_tensor("k_scr", [PAIRS, 128, S], BF16, kind="Internal")
    v_dram = nc.dram_tensor("v_scr", [PAIRS, 128, 2, NKC, 66], BF16, kind="Internal")
    x1_dram = nc.dram_tensor("x1_scr", [NT, 128, D], F32, kind="ExternalInput" if p3_only else "Internal")
    xs_cp = nc.dram_tensor("xs_cp", [E * CAP, D], BF16, kind="Internal")
    y_cp = nc.dram_tensor("y_cp", [E * CAP, D], BF16, kind="Internal")
    F_dram = nc.dram_tensor("F_scr", [1, 128, 8192], F32, kind="Internal")

    RW = nc.alloc_sbuf_tensor("RW", [128, 24576], BF16)
    RK = nc.alloc_sbuf_tensor("RK", [128, 16384], BF16)
    RV = nc.alloc_sbuf_tensor("RV", [128, 17408], BF16)
    RQ = nc.alloc_sbuf_tensor("RQ", [128, 16384], BF16)
    RT = nc.alloc_sbuf_tensor("RT", [128, 16384], BF16)
    RS = nc.alloc_sbuf_tensor("RS", [128, 4096], F32)
    sb_s = nc.alloc_sbuf_tensor("sb_s", [128, D], F32)
    sb_b = nc.alloc_sbuf_tensor("sb_b", [128, D], F32)
    stats_all = nc.alloc_sbuf_tensor("stats_all", [128, 2 * NSEC * 8], F32)
    r_all = nc.alloc_sbuf_tensor("r_all", [128, NSEC * 8], F32)
    id_bf = nc.alloc_sbuf_tensor("id_bf", [128, 128], BF16)
    id_f = nc.alloc_sbuf_tensor("id_f", [128, 128], F32)
    gate_f = nc.alloc_sbuf_tensor("gate_f", [128, 8 * E], F32)
    gate_hi = nc.alloc_sbuf_tensor("gate_hi", [128, 8 * E], BF16)
    gate_lo = nc.alloc_sbuf_tensor("gate_lo", [128, 8 * E], BF16)
    shift_t = nc.alloc_sbuf_tensor("shift_t", [128, 1], F32)
    se_all = nc.alloc_sbuf_tensor("se_all", [128, NT * E], F32)
    bn6 = nc.alloc_sbuf_tensor("bn6", [128, 12], F32)
    agg2 = nc.alloc_sbuf_tensor("agg2", [128, 2], F32)
    rt1 = nc.alloc_sbuf_tensor("rt1", [128, 1], F32)
    rt2 = nc.alloc_sbuf_tensor("rt2", [128, 1], F32)
    rt1b = nc.alloc_sbuf_tensor("rt1b", [128, 1], F32)
    agg2b = nc.alloc_sbuf_tensor("agg2b", [128, 2], F32)
    p25r = nc.alloc_sbuf_tensor("p25r", [128, 2], F32)
    aggA = nc.alloc_sbuf_tensor("aggA", [128, 2], F32)
    rtA = nc.alloc_sbuf_tensor("rtA", [128, 1], F32)
    lsb = nc.alloc_sbuf_tensor("lsb", [128, E], F32)
    tvb = nc.alloc_sbuf_tensor("tvb", [128, 1], F32)
    ohb = nc.alloc_sbuf_tensor("ohb", [128, E], F32)
    scb = nc.alloc_sbuf_tensor("scb", [128, 1], F32)
    rden = nc.alloc_sbuf_tensor("rden", [128, 1], F32)
    iota_sb = nc.alloc_sbuf_tensor("iota_sb", [128, 1], F32)
    ebase_b = nc.alloc_sbuf_tensor("ebase_b", [128, E], F32)
    oh_all = nc.alloc_sbuf_tensor("oh_all", [128, NT * E], F32)
    score_all = nc.alloc_sbuf_tensor("score_all", [128, NT], F32)
    idx_all = nc.alloc_sbuf_tensor("idx_all", [128, NT], mybir.dt.int32)
    pfx8 = nc.alloc_sbuf_tensor("pfx8", [128, E], F32)
    tmp8 = nc.alloc_sbuf_tensor("tmp8", [128, E], F32)
    slotf = nc.alloc_sbuf_tensor("slotf", [128, 1], F32)


    PA = nc.alloc_psum_tensor("PA", [128, 1024], F32)
    PB = nc.alloc_psum_tensor("PB", [128, 1024], F32)
    PC = nc.alloc_psum_tensor("PC", [128, 1024], F32)
    PD = nc.alloc_psum_tensor("PD", [128, 1024], F32)
    PTa = PD[:, 0:128]
    PTb = PD[:, 512:640]

    pl = Pl(nc)
    W, DO, DO0 = pl.w, pl.do, pl.do0
    ptg = {"a": 0, "b": 0}
    g_kq = {0: 0, 1: 0}
    g_v = {0: 0, 1: 0, 2: 0, 3: 0}
    g_ep = {0: 0, 1: 0}

    wq_v = RW[:, 0:8192].rearrange("p (c m) -> p c m", c=8)
    wk_v = RW[:, 8192:16384].rearrange("p (c m) -> p c m", c=8)
    wv_v = RW[:, 16384:24576].rearrange("p (c m) -> p c m", c=8)
    xnfm = [RK[:, 0:8192].rearrange("p (c n) -> p c n", c=8),
            RK[:, 8192:16384].rearrange("p (c n) -> p c n", c=8)]
    rv32 = RV[:, 0:8192].bitcast(F32)
    xstage = [rv32[:, 0:1024], rv32[:, 1024:2048]]
    xnf32 = rv32[:, 2048:3072]
    xnf32b = rv32[:, 3072:4096]
    kevac = [RV[:, 12288:13312], RV[:, 13312:14336]]
    vstage = RV[:, 14336:16448].rearrange("p (h t n) -> p h t n", h=4, t=8)
    q_fm = RQ[:].rearrange("p (r n) -> p r n", r=PAIRS)
    kbuf = [RK[:, 0:8192], RK[:, 8192:16384]]
    vbuf = [RV[:, 0:8448].rearrange("p (h t n) -> p h t n", h=2, t=NKC),
            RV[:, 8448:16896].rearrange("p (h t n) -> p h t n", h=2, t=NKC)]
    probsA = [RS[:, 0:512].bitcast(BF16), RS[:, 512:1024].bitcast(BF16)]
    probsB = [RS[:, 1024:1536].bitcast(BF16), RS[:, 1536:2048].bitcast(BF16)]
    pvs = RW[:, 12288:14336].bitcast(F32)[0:65, :]
    p25b = RW[:, 14336:16384].bitcast(F32)
    attn_tm = RT[:].rearrange("p (t n) -> p t n", t=NT)
    st0 = RS[:, 0:1024]
    st1 = RS[:, 1024:2048]
    scr1 = RS[:, 2048:3072]
    scr2 = RS[:, 3072:4096]
    scr2bf = RS[:, 3072:3584].bitcast(BF16)
    h_tm = RK[:].rearrange("p (t n) -> p t n", t=NT)
    h_fm = RW[:, 0:8192].rearrange("p (c n) -> p c n", c=8)
    wg_buf = [RW[:, 8192:9216].rearrange("p (c m) -> p c m", c=8),
              RW[:, 9216:10240].rearrange("p (c m) -> p c m", c=8),
              RW[:, 20480:21504].rearrange("p (c m) -> p c m", c=8),
              RW[:, 21504:22528].rearrange("p (c m) -> p c m", c=8)]
    wu_buf = [RW[:, 10240:11264].rearrange("p (c m) -> p c m", c=8),
              RW[:, 11264:12288].rearrange("p (c m) -> p c m", c=8),
              RW[:, 22528:23552].rearrange("p (c m) -> p c m", c=8),
              RW[:, 23552:24576].rearrange("p (c m) -> p c m", c=8)]
    xs_fm = RW[:, 12288:20480].rearrange("p (c n) -> p c n", c=8)
    lo_fm = xs_fm
    wd_v = RV[:, 0:16384].rearrange("p (i m) -> p i m", i=16)
    xs_tm_e = RW[:, 12288:16384].bitcast(F32).rearrange("p (q m) -> p q m", q=2)
    xs_fm_e = RW[:, 16384:20480].rearrange("p (c n) -> p c n", c=8)
    y_tm_e = RK[:, 0:4096].rearrange("p (q m) -> p q m", q=4)
    scrF = [rv32[:, 0:1024].rearrange("p (c n) -> p c n", c=8),
            rv32[:, 1024:2048].rearrange("p (c n) -> p c n", c=8)]
    F_v = RQ[:].bitcast(F32).rearrange("p (c n) -> p c n", c=8)
    act_v = RT[:].rearrange("p (i n) -> p i n", i=16)
    act_sp_v = RT[:, 0:8192].rearrange("p (i n) -> p i n", i=16)
    oh_fm8 = RS[0:8, 0:2048]
    pfx_fm8 = RS[0:8, 2048:4096]

    DO("sync", lambda h: h.dma_start(sb_s[:], lns_d.ap().unsqueeze(0).partition_broadcast(128)), "dm_const", 16)
    DO("sync", lambda h: h.dma_start(sb_b[:], lnb_d.ap().unsqueeze(0).partition_broadcast(128)), "dm_const", 16)
    DO("sync", lambda h: h.dma_start(id_bf[:], idb_d[:]), "dm_const", 16)
    DO("sync", lambda h: h.dma_start(id_f[:], idf_d[:]), "dm_const", 16)
    DO("sync", lambda h: h.dma_start(
        gate_f[:].rearrange("p (c e) -> p c e", c=8),
        gw_d.ap().rearrange("(c p) e -> p c e", p=128)), "dm_const", 16)
    DO("sync", lambda h: h.dma_start(iota_sb[:], iota_d[:]), "dm_const", 16)
    DO("sync", lambda h: h.dma_start(ebase_b[:], ebase_d.ap().unsqueeze(0).partition_broadcast(128)), "dm_const", 16)
    DO("gpsimd", lambda h: h.dma_start(wq_v, wq_d.ap().rearrange("(c p) m -> p c m", p=128)), "dm_w", 16)
    DO("gpsimd", lambda h: h.dma_start(wk_v, wk_d.ap().rearrange("(c p) m -> p c m", p=128)), "dm_w", 16)
    DO("gpsimd", lambda h: h.dma_start(wv_v, wv_d.ap().rearrange("(c p) m -> p c m", p=128)), "dm_w", 16)
    m_const_dma = pl.c["dm_const"]
    m_w_dma = pl.c["dm_w"]

    W("dve", "dm_const", m_const_dma)
    W("dve", "dm_w", m_w_dma)
    DO("dve", lambda h: h.memset(shift_t[:], -CSHIFT), "dve")
    DO("dve", lambda h: h.tensor_copy(gate_hi[:], gate_f[:]), "dve")
    m = pl.c["dve"]
    W("dve", "dve", m)
    DO("dve", lambda h: h.tensor_sub(gate_lo[:], gate_f[:], gate_hi[:]), "dve")
    DO("dve", lambda h: h.memset(vstage[:, :, :, 64:66], 0.0), "dve")
    m = pl.c["dve"]
    W("dve", "dve", m)
    DO("dve", lambda h: h.memset(vstage[:, :, :, 64:65], 1.0), "dve")
    m_init_dve = pl.c["dve"]
    DO("dve", lambda h: h.memset(kevac[0], 0.0), "dve")
    mz = pl.c["dve"]
    W("gpsimd", "dve", mz)
    for rr in range(E * CAP // 128):
        DO("gpsimd", lambda h, rr=rr: h.dma_start(
            xs_cp[rr * 128:(rr + 1) * 128, :], kevac[0]), "dm_z", 16)
    m_zfill = pl.c["dm_z"]

    if upto < 1:
        return nc, pl, dict(locals())

    if not p3_only:
        xn_done_dve = [0] * NSEC
        proj_done_pe = [0] * NSEC
        load_cnt = {}
        tr_evac_dve = {}

        def p1_load(sec, t, par):
            g = sec * 8 + t
            W("sync", "dve", load_cnt.get(("cons", g - 2), 0))
            load_cnt[("ld", g)] = DO("sync", lambda h, g=g, par=par: h.dma_start(
                xstage[par], x_d[g * 128:(g + 1) * 128, :]), f"dm_x{par}", 16)

        def p1_stats(sec, t, par):
            g = sec * 8 + t
            W("dve", f"dm_x{par}", load_cnt[("ld", g)])
            DO("dve", lambda h, par=par: h.bn_stats(bn6[:, 0:6], xstage[par][:, 0:512]), "dve")
            DO("dve", lambda h, par=par: h.bn_stats(bn6[:, 6:12], xstage[par][:, 512:1024]), "dve")
            m = pl.c["dve"]
            W("dve", "dve", m)
            DO("dve", lambda h, g=g: h.bn_aggr(stats_all[:, 2 * g:2 * g + 2], bn6[:]), "dve")
            m = pl.c["dve"]
            W("dve", "dve", m)
            rt = rt1 if par == 0 else rt1b
            DO("dve", lambda h, g=g, rt=rt: h.tensor_scalar(
                rt[:], stats_all[:, 2 * g + 1:2 * g + 2], COMB_A, COMB_B, ALU.mult, ALU.add), "dve")
            m = pl.c["dve"]
            W("act", "dve", m)
            W("act", "act", pl.c["act"])
            DO("act", lambda h, rt=rt: h.activation(rt2[:], rt[:], AF.Ln), "act")
            m = pl.c["act"]
            W("act", "act", m)
            DO("act", lambda h, g=g: h.activation(
                r_all[:, g:g + 1], rt2[:], AF.Exp, scale=-0.5), "act")
            return pl.c["act"]

        def p1_xn(sec, t, par, r_ready_act):
            g = sec * 8 + t
            sp = sec % 2
            if t == 0 and sec >= 2:
                W("dve", "pe", proj_done_pe[sec - 2])
            W("dve", "act", r_ready_act)
            W("dve", "pe", tr_evac_dve.get(("xnb", g - 1), 0))
            DO("dve", lambda h, par=par, g=g: h.tensor_scalar(
                xnf32[:], xstage[par], stats_all[:, 2 * g:2 * g + 1],
                r_all[:, g:g + 1], ALU.subtract, ALU.mult), "dve")
            load_cnt[("cons", g)] = pl.c["dve"]
            m = pl.c["dve"]
            W("dve", "dve", m)
            DO("dve", lambda h: h.tensor_mul(xnf32[:], xnf32[:], sb_s[:]), "dve")
            m = pl.c["dve"]
            W("dve", "dve", m)
            DO("dve", lambda h: h.tensor_add(xnf32b[:], xnf32[:], sb_b[:]), "dve")
            m_xnbf = pl.c["dve"]
            tr_evac_dve[("xnb", g)] = None
            for c in range(8):
                key = "a" if c % 2 == 0 else "b"
                ptt = PTa if c % 2 == 0 else PTb
                W("pe", "dve", m_xnbf)
                W("pe", "dve", ptg[key])
                DO("pe", lambda h, ptt=ptt, c=c: h.transpose(
                    ptt, xnf32b[:, c * 128:(c + 1) * 128], id_f[:]), "pe")
                mpe = pl.c["pe"]
                W("dve", "pe", mpe)
                ptg[key] = DO("dve", lambda h, sp=sp, c=c, t=t, ptt=ptt: h.tensor_copy(
                    xnfm[sp][:, c, t * 128:(t + 1) * 128], ptt), "dve")
            tr_evac_dve[("xnb", g)] = pl.c["pe"]
            return pl.c["dve"]

        kq_psum = [PA, PB]
        v_psum = [PC[:, 0:256], PC[:, 512:768], PB[:, 0:256], PB[:, 512:768]]
        kdma_last = {0: 0, 1: 0}
        vdma_last = [0]
        W("pe", "dm_w", m_w_dma)
        W("pe", "dm_const", m_const_dma)

        kv_evac = {}

        def wait_g(eng, gk):
            if isinstance(gk, tuple):
                W(eng, gk[0], gk[1])
            else:
                W(eng, "dve", gk)

        def proj_chunk(psec, i):
            spp = psec % 2
            ps = kq_psum[i % 2]
            W("pe", "dve", xn_done_dve[psec])
            wait_g("pe", g_kq[i % 2])
            for c in range(8):
                for j in range(2):
                    DO("pe", lambda h, ps=ps, c=c, j=j, i=i, spp=spp: h.matmul(
                        ps[:, j * 512:(j + 1) * 512],
                        wk_v[:, c, i * 128:(i + 1) * 128],
                        xnfm[spp][:, c, j * 512:(j + 1) * 512],
                        start=(c == 0), stop=(c == 7)), "pe")
            mpe = pl.c["pe"]
            if psec == 0 and i == 0:
                W("act", "dm_z", m_zfill)
            W("act", "pe", mpe)
            W("act", f"dm_k{i % 2}", kdma_last[i % 2])
            kv_evac[i] = DO("act", lambda h, ps=ps: h.copy(
                kevac[i % 2], ps[:]), "act")
            g_kq[i % 2] = ("act", kv_evac[i])
            W("gpsimd", "act", kv_evac[i])
            load_cnt[("kdma", i)] = DO("gpsimd", lambda h, i=i, psec=psec: h.dma_start(
                k_dram[i, :, psec * 1024:(psec + 1) * 1024], kevac[i % 2]), f"dm_k{i % 2}", 16)
            kdma_last[i % 2] = load_cnt[("kdma", i)]
            grp, thalf = i // 2, (i % 2) * 4
            for t in range(thalf, thalf + 4):
                ps = v_psum[t % 4]
                if t % 4 >= 2:
                    wait_g("pe", g_kq[1])
                wait_g("pe", g_v[t % 4])
                for c in range(8):
                    DO("pe", lambda h, ps=ps, c=c, grp=grp, t=t, spp=spp: h.matmul(
                        ps, xnfm[spp][:, c, t * 128:(t + 1) * 128],
                        wv_v[:, c, grp * 256:(grp + 1) * 256],
                        start=(c == 0), stop=(c == 7)), "pe")
                mpe = pl.c["pe"]
                W("act", "pe", mpe)
                W("act", "dm_v", vdma_last[0] if t <= 1 else 0)
                kv_evac[("v", grp, t)] = DO("act", lambda h, ps=ps, t=t: h.copy(
                    vstage[:, :, t, 0:64],
                    ps.rearrange("p (hh n) -> p hh n", hh=4)), "act")
                g_v[t % 4] = ("act", kv_evac[("v", grp, t)])
                if t % 4 >= 2:
                    g_kq[1] = ("act", kv_evac[("v", grp, t)])
            if i % 2 == 1:
                W("gpsimd", "act", kv_evac[("v", grp, 7)])
                for qq in range(2):
                    vdma_last[0] = DO("gpsimd", lambda h, grp=grp, psec=psec, qq=qq: h.dma_start(
                        v_dram[2 * grp + qq, :, :, psec * 8:psec * 8 + 8, :],
                        vstage[:, 2 * qq:2 * qq + 2, :, :]), "dm_v", 16)
            if psec < 2:
                ps = kq_psum[i % 2]
                wait_g("pe", g_kq[i % 2])
                for c in range(8):
                    for j in range(2):
                        DO("pe", lambda h, ps=ps, c=c, j=j, i=i, spp=spp: h.matmul(
                            ps[:, j * 512:(j + 1) * 512],
                            wq_v[:, c, i * 128:(i + 1) * 128],
                            xnfm[spp][:, c, j * 512:(j + 1) * 512],
                            start=(c == 0), stop=(c == 7)), "pe")
                mpe = pl.c["pe"]
                W("act", "pe", mpe)
                kv_evac[("q", i)] = DO("act", lambda h, ps=ps, i=i, psec=psec: h.copy(
                    q_fm[:, i, psec * 1024:(psec + 1) * 1024], ps[:]), "act")
                g_kq[i % 2] = ("act", kv_evac[("q", i)])

        for sec in range(NSEC):
            sp = sec % 2
            r_acts = {}
            for t in range(8):
                par = t % 2
                p1_load(sec, t, par)
                r_acts[t] = p1_stats(sec, t, par)
                if t >= 1:
                    p1_xn(sec, t - 1, (t - 1) % 2, r_acts[t - 1])
                if sec >= 1:
                    proj_chunk(sec - 1, t)
            xn_done_dve[sec] = p1_xn(sec, 7, 1, r_acts[7])
            if sec >= 1:
                proj_done_pe[sec - 1] = pl.c["pe"]
        for i in range(8):
            proj_chunk(NSEC - 1, i)
        proj_done_pe[NSEC - 1] = pl.c["pe"]

        sn1 = pl.barrier()

        if upto < 2:
            return nc, pl, dict(locals())

        x1_spill = {}
        p25_gate = [0]
        p25st = [RW[:, 16384:18432].bitcast(F32), RW[:, 18432:20480].bitcast(F32)]
        scr2p = [scr2, p25b]
        aggp = [agg2, agg2b]
        rtp = [rt1, rt1b]

        def p25_front_dve(t):
            par = t % 2
            W("sync", "dve", x1_spill.get(("cons", t - 2), 0))
            mld = DO("sync", lambda h, t=t, par=par: h.dma_start(
                p25st[par], x_d[t * 128:(t + 1) * 128, :]), f"dm_x{par}", 16)
            W("dve", f"dm_x{par}", mld)
            W("dve", "dve", p25_gate[0])
            DO("dve", lambda h, t=t, par=par: h.tensor_scalar(
                scr1[:], p25st[par], stats_all[:, 2 * t:2 * t + 1],
                r_all[:, t:t + 1], ALU.subtract, ALU.mult), "dve")
            m = pl.c["dve"]; W("dve", "dve", m)
            DO("dve", lambda h: h.tensor_mul(scr1[:], scr1[:], sb_s[:]), "dve")
            m = pl.c["dve"]; W("dve", "dve", m)
            DO("dve", lambda h: h.tensor_add(scr1[:], scr1[:], sb_b[:]), "dve")
            m = pl.c["dve"]; W("dve", "dve", m)
            W("dve", "dm_x1w" if par == 0 else "dm_x1w1", x1_spill.get(("dma", t - 2), 0))
            DO("dve", lambda h, t=t, par=par: h.tensor_add(scr2p[par][:], attn_tm[:, t, :], scr1[:]), "dve")
            m = pl.c["dve"]; W("dve", "dve", m)
            DO("dve", lambda h, par=par: h.bn_stats(bn6[:, 0:6], scr2p[par][:, 0:512]), "dve")
            DO("dve", lambda h, par=par: h.bn_stats(bn6[:, 6:12], scr2p[par][:, 512:1024]), "dve")
            m = pl.c["dve"]; W("dve", "dve", m)
            DO("dve", lambda h, par=par: h.bn_aggr(aggp[par][:], bn6[:]), "dve")
            m = pl.c["dve"]; W("dve", "dve", m)
            DO("dve", lambda h, par=par: h.tensor_scalar(
                rtp[par][:], aggp[par][:, 1:2], 1.0, LN_E1, ALU.mult, ALU.add), "dve")
            return pl.c["dve"]

        def p25_front_act(t, dve_cnt):
            par = t % 2
            W("act", "dve", dve_cnt)
            W("act", "act", pl.c["act"])
            DO("act", lambda h, par=par: h.activation(rt2[:], rtp[par][:], AF.Ln), "act")
            m = pl.c["act"]; W("act", "act", m)
            DO("act", lambda h, par=par: h.activation(
                p25r[:, par:par + 1], rt2[:], AF.Exp, scale=-0.5), "act")
            return pl.c["act"]

        def p25_front(t):
            return p25_front_act(t, p25_front_dve(t))

        def p25_back(t, act_cnt):
            par = t % 2
            W("dve", "act", act_cnt)
            DO("dve", lambda h, par=par: h.tensor_scalar(
                scr2p[par][:], scr2p[par][:], aggp[par][:, 0:1],
                p25r[:, par:par + 1], ALU.subtract, ALU.mult), "dve")
            m = pl.c["dve"]; W("dve", "dve", m)
            DO("dve", lambda h, par=par: h.tensor_add(scr2p[par][:], scr2p[par][:], p25st[par]), "dve")
            x1_spill[("cons", t)] = pl.c["dve"]
            m = pl.c["dve"]
            W("sync", "dve", m)
            x1_spill[("dma", t)] = DO("sync", lambda h, t=t, par=par: h.dma_start(
                x1_dram[t, :, :], scr2p[par][:]), "dm_x1w" if par == 0 else "dm_x1w1", 16)

        acts25 = {}
        dve25 = {}

        def emit_p25(half, attn_dve_done):
            p25_gate[0] = attn_dve_done
            ts_ = range(half * 8, half * 8 + 8)
            for t in ts_:
                acts25[t] = p25_front(t)
                if t > half * 8:
                    p25_back(t - 1, acts25[t - 1])
            p25_back(ts_[-1], acts25[ts_[-1]])

        def p25_stage(p):
            if 0 <= p - 2 <= 7:
                p25_back(p - 2, acts25[p - 2])
            if 0 <= p - 1 <= 7:
                acts25[p - 1] = p25_front_act(p - 1, dve25[p - 1])
            if p <= 7:
                dve25[p] = p25_front_dve(p)

        kv_load = {}

        def load_pair(idx):
            pair, par = idx % PAIRS, idx % 2
            W("sync", "pe", kv_load.get(("consumed", idx - 2), 0))
            DO("sync", lambda h, pair=pair, par=par: h.dma_start(
                kbuf[par], k_dram[pair, :, :]), f"dm_kv{par}", 16)
            kv_load[("ld", idx)] = DO("sync", lambda h, pair=pair, par=par: h.dma_start(
                vbuf[par], v_dram[pair, :, :, :, :]), f"dm_kv{par}", 16)

        load_pair(0)
        load_pair(1)

        pvm = {}
        prevexp = {"A": 0, "B": 0}
        prevpv = {("A", k): 0 for k in range(4)}
        prevpv.update({("B", k): 0 for k in range(4)})
        pv_ps = {"A": PC[0:65, :], "B": PD[0:65, :]}
        sc_ps = {"A": PA, "B": PB}
        probs = {"A": [probsA[0], probsA[1],
                       RW[:, 20480:21504], RW[:, 21504:22528]],
                 "B": [probsB[0], probsB[1],
                       RW[:, 22528:23552], RW[:, 23552:24576]]}
        pvsAB = RW[:, 8192:12288].bitcast(F32)[0:65, :]
        d_tps = [PC[:, 0:65], PC[:, 512:577], PD[:, 0:65], PD[:, 512:577]]
        pending = [None]
        dtr = {}
        last_tr_pe = [0]

        def defer_step(s):
            pd_ = pending[0]
            hh = 0 if s < 8 else 1
            t = s % 8
            tg = pd_["qh"] * 8 + t
            head = 2 * pd_["pair"] + hh
            tps = d_tps[s % 4]
            W("pe", "dve", pd_["gate"])
            W("pe", "dve", dtr.get(("ts", s - 4), 0))
            DO("pe", lambda h, tps=tps, hh=hh, t=t: h.transpose(
                tps,
                pvsAB[:, hh * 1024 + t * 128: hh * 1024 + (t + 1) * 128],
                id_f[0:65, 0:65]), "pe")
            last_tr_pe[0] = pl.c["pe"]
            mtr = pl.c["pe"]
            W("dve", "pe", mtr)
            W("dve", "dve", pl.c["dve"])
            DO("dve", lambda h, tps=tps: h.reciprocal(rden[:], tps[:, 64:65]), "dve")
            mr = pl.c["dve"]
            W("dve", "dve", mr)
            dtr[("ts", s)] = DO("dve", lambda h, tps=tps, tg=tg, head=head: h.tensor_scalar(
                attn_tm[:, tg, head * 64:(head + 1) * 64],
                tps[:, 0:64], rden[:], None, ALU.mult), "dve")

        def emit_pv(HH, kc, par, hh, start, stop):
            for j in range(2):
                DO("pe", lambda h, HH=HH, par=par, hh=hh, kc=kc, j=j, start=start, stop=stop: h.matmul(
                    pv_ps[HH][:, j * 512:(j + 1) * 512],
                    vbuf[par][:, hh, kc, 0:65],
                    probs[HH][kc % 4][:, j * 512:(j + 1) * 512],
                    start=start, stop=stop), "pe")
            pvm[(HH, kc)] = pl.c["pe"]

        for qh in range(2):
            for pair in range(PAIRS):
                par = pair % 2
                act_base = pl.c["act"]
                extra_act = 2 if (qh == 1 and pair >= 2) else 0
                expd = {}
                for kc in range(NKC):
                    expd[("A", kc)] = act_base + extra_act + 2 * kc + 1
                    expd[("B", kc)] = act_base + extra_act + 2 * kc + 2
                W("pe", f"dm_kv{par}", kv_load[("ld", qh * PAIRS + pair)])
                scm = {}
                for kc in range(NKC):
                    for hh, HH in ((0, "A"), (1, "B")):
                        h0 = 64 * hh
                        W("pe", "act", expd.get((HH, kc - 1), prevexp[HH]))
                        for j in range(2):
                            DO("pe", lambda h, HH=HH, pair=pair, par=par, h0=h0, kc=kc, qh=qh, j=j: h.matmul(
                                sc_ps[HH][:, j * 512:(j + 1) * 512],
                                kbuf[par][h0:h0 + 64, kc * 128:(kc + 1) * 128],
                                q_fm[h0:h0 + 64, pair, qh * 1024 + j * 512: qh * 1024 + (j + 1) * 512],
                                start=True, stop=True), "pe")
                        scm[(HH, kc)] = pl.c["pe"]
                    if kc <= 3 and pending[0] is not None:
                        for s4 in range(4):
                            defer_step(4 * kc + s4)
                    if kc == 4:
                        if qh == 1 and pair >= 1:
                            if pair == 1:
                                p25_gate[0] = pl.c["dve"]
                            p25_stage(pair - 1)
                    if kc >= 4:
                        kcl = kc - 4
                        for hh, HH in ((0, "A"), (1, "B")):
                            if kcl == 0:
                                W("pe", "dve", dtr.get(("ts", 13 if HH == "A" else 15), 0))
                            emit_pv(HH, kcl, par, hh, start=(kcl == 0), stop=False)
                for kc in range(NKC):
                    for HH in ("A", "B"):
                        W("act", "pe", scm[(HH, kc)])
                        W("act", "pe", pvm.get((HH, kc - 4), prevpv[(HH, kc % 4)]))
                        got = DO("act", lambda h, HH=HH, kc=kc: h.activation(
                            probs[HH][kc % 4], sc_ps[HH][:], AF.Exp,
                            bias=shift_t[:], scale=ATT_SCALE), "act")
                        assert got == expd[(HH, kc)]
                for kcl in range(NKC - 4, NKC):
                    for hh, HH in ((0, "A"), (1, "B")):
                        W("pe", "act", expd[(HH, kcl)])
                        emit_pv(HH, kcl, par, hh, start=False, stop=(kcl == NKC - 1))
                for HH in ("A", "B"):
                    prevexp[HH] = expd[(HH, NKC - 1)]
                    for k in range(4):
                        prevpv[(HH, k)] = pvm[(HH, NKC - 4 + k)]
                W("dve", "pe", pvm[("A", NKC - 1)])
                W("dve", "pe", pvm[("B", NKC - 1)])
                W("dve", "pe", last_tr_pe[0])
                DO("dve", lambda h: h.tensor_copy(pvsAB[:, 0:1024], PC[0:65, :]), "dve")
                gB = DO("dve", lambda h: h.tensor_copy(pvsAB[:, 1024:2048], PD[0:65, :]), "dve")
                pending[0] = {"qh": qh, "pair": pair, "gate": gB}
                kv_load[("consumed", qh * PAIRS + pair)] = pl.c["pe"]
                if qh * PAIRS + pair + 2 < 2 * PAIRS:
                    load_pair(qh * PAIRS + pair + 2)

        for s in range(16):
            defer_step(s)
        pending[0] = None
        p25_stage(7)
        p25_stage(8)
        p25_stage(9)
        sn2 = pl.barrier()

        if upto < 3:
            return nc, pl, dict(locals())

        sn25 = pl.barrier()

    if upto < 4:
        return nc, pl, dict(locals())

    out_dma = {}
    wbuf_used = {}
    gu_used = {}
    ydone = {}
    wdma = {}
    mshared_done = [0, 0]

    def logit_tile(t, half, gate0=0):
        tl = (t - half * 8) * 128
        W("pe", "dve", max(wbuf_used.get(("lg", t - 1), 0), gate0))
        for c in range(8):
            DO("pe", lambda h, c=c, tl=tl: h.matmul(
                PC[:, 0:8], h_fm[:, c, tl:tl + 128],
                gate_hi[:].rearrange("p (c e) -> p c e", c=8)[:, c, :],
                start=(c == 0), stop=False), "pe")
            DO("pe", lambda h, c=c, tl=tl: h.matmul(
                PC[:, 0:8], h_fm[:, c, tl:tl + 128],
                gate_lo[:].rearrange("p (c e) -> p c e", c=8)[:, c, :],
                start=False, stop=False), "pe")
            DO("pe", lambda h, c=c, tl=tl: h.matmul(
                PC[:, 0:8], lo_fm[:, c, tl:tl + 128],
                gate_hi[:].rearrange("p (c e) -> p c e", c=8)[:, c, :],
                start=False, stop=(c == 7)), "pe")
        mlg = pl.c["pe"]
        wbuf_used[("lgpe", t)] = mlg
        W("dve", "pe", mlg)
        DO("dve", lambda h: h.tensor_copy(lsb[:], PC[:, 0:8]), "dve")
        wbuf_used[("lg", t)] = pl.c["dve"]
        m = pl.c["dve"]; W("dve", "dve", m)
        DO("dve", lambda h: h.reduce_max(tvb[:], lsb[:], AX.X), "dve")
        m = pl.c["dve"]; W("dve", "dve", m)
        DO("dve", lambda h, t=t: h.tensor_scalar(
            oh_all[:, t * E:(t + 1) * E], lsb[:], tvb[:], None, ALU.is_equal), "dve")
        m = pl.c["dve"]; W("act", "dve", m)
        W("act", "act", pl.c["act"])
        DO("act", lambda h, t=t: h.activation(
            score_all[:, t:t + 1], tvb[:], AF.Sigmoid), "act")

    def emit_3a(half, tr_only=False):
        tiles = range(half * 8, half * 8 + 8)
        for t in tiles:
            par = t % 2
            if half == 1 and t == 8:
                W("dve", "dm_x1w", x1_spill.get(("dma", 14), 0))
                W("dve", "dm_x1w1", x1_spill.get(("dma", 15), 0))
            W("sync", "dve", wbuf_used.get(("x1cons", t - 2), 0))
            W("sync", "dm_x1w" if par == 0 else "dm_x1w1", x1_spill.get(("dma", t), 0))
            mld = DO("sync", lambda h, t=t, par=par: h.dma_start(
                [st0, st1][par], x1_dram[t, :, :]), f"dm_x1r{par}", 16)
            W("dve", f"dm_x1r{par}", mld)
            DO("dve", lambda h, par=par: h.bn_stats(bn6[:, 0:6], [st0, st1][par][:, 0:512]), "dve")
            DO("dve", lambda h, par=par: h.bn_stats(bn6[:, 6:12], [st0, st1][par][:, 512:1024]), "dve")
            m = pl.c["dve"]; W("dve", "dve", m)
            DO("dve", lambda h: h.bn_aggr(agg2[:], bn6[:]), "dve")
            m = pl.c["dve"]; W("dve", "dve", m)
            DO("dve", lambda h: h.tensor_scalar(rt1[:], agg2[:, 1:2], 1.0, LN_E1, ALU.mult, ALU.add), "dve")
            m = pl.c["dve"]; W("act", "dve", m)
            W("act", "act", pl.c["act"])
            DO("act", lambda h: h.activation(rt2[:], rt1[:], AF.Ln), "act")
            m = pl.c["act"]; W("act", "act", m)
            DO("act", lambda h: h.activation(rt1[:], rt2[:], AF.Exp, scale=-0.5), "act")
            m = pl.c["act"]; W("dve", "act", m)
            W("dve", "pe", wbuf_used.get(("htr", t - 1), 0))
            DO("dve", lambda h, par=par: h.tensor_scalar(
                scr1[:], [st0, st1][par], agg2[:, 0:1], rt1[:], ALU.subtract, ALU.mult), "dve")
            wbuf_used[("x1cons", t)] = pl.c["dve"]
            m = pl.c["dve"]; W("dve", "dve", m)
            DO("dve", lambda h, t=t: h.tensor_copy(h_tm[:, t, :], scr1[:]), "dve")
            m = pl.c["dve"]; W("dve", "dve", m)
            DO("dve", lambda h, t=t: h.tensor_sub(scr2[:], scr1[:], h_tm[:, t, :]), "dve")
            mlo = pl.c["dve"]
            for c in range(8):
                keys = ("a", "b") if c % 2 == 0 else ("b", "a")
                pts = (PTa, PTb) if c % 2 == 0 else (PTb, PTa)
                W("pe", "dve", mlo)
                W("pe", "dve", ptg[keys[0]])
                W("pe", "dve", ptg[keys[1]])
                DO("pe", lambda h, pts=pts, t=t, c=c: h.transpose(
                    pts[0], scr1[:, c * 128:(c + 1) * 128], id_f[:]), "pe")
                DO("pe", lambda h, pts=pts, c=c: h.transpose(
                    pts[1], scr2[:, c * 128:(c + 1) * 128], id_f[:]), "pe")
                mpe = pl.c["pe"]
                W("dve", "pe", mpe)
                ptg[keys[0]] = DO("dve", lambda h, pts=pts, c=c, t=t, half=half: h.tensor_copy(
                    h_fm[:, c, (t - half * 8) * 128:(t - half * 8 + 1) * 128], pts[0]), "dve")
                ptg[keys[1]] = DO("dve", lambda h, pts=pts, c=c, t=t, half=half: h.tensor_copy(
                    lo_fm[:, c, (t - half * 8) * 128:(t - half * 8 + 1) * 128], pts[1]), "dve")
            wbuf_used[("htr", t)] = pl.c["pe"]
            mtr = pl.c["dve"]
            if not tr_only:
                logit_tile(t, half, gate0=mtr)

    def emit_shared(half, pre_memset=0, ins=None, ins_down=None, predown_fn=None,
                    act_gate_fn=None):
        e = "sh"
        mh_entry = pl.c["dve"]
        if pre_memset:
            W("dve", "dm_x1w", pre_memset)
        DO("dve", lambda h: h.memset(RQ[:].bitcast(F32), 0.0), "dve")
        W("pe", "dve", mh_entry)
        W("gpsimd", "pe", wdma.get(("shwd_cons",), 0))
        wdma[("wd", e)] = DO("gpsimd", lambda h: h.dma_start(
            wd_v, shd_d.ap().rearrange("(i p) m -> p i m", p=128)), "dm_wd", 16)
        for i in range(16):
            bufp = i % 4
            W("gpsimd", "pe", gu_used.get(("u", e, i - 4), 0))
            wdma[("g", e, i)] = DO("gpsimd", lambda h, i=i, bufp=bufp: h.dma_start(
                wg_buf[bufp], shg_d.ap().rearrange(
                    "(c p) m -> p c m", p=128)[:, :, i * 128:(i + 1) * 128]), f"dm_wg{bufp}", 16)
            wdma[("u", e, i)] = DO("gpsimd", lambda h, i=i, bufp=bufp: h.dma_start(
                wu_buf[bufp], shu_d.ap().rearrange(
                    "(c p) m -> p c m", p=128)[:, :, i * 128:(i + 1) * 128]), f"dm_wg{bufp}", 16)
            W("pe", f"dm_wg{bufp}", wdma[("u", e, i)])
            W("pe", "act", gu_used.get(("prev_silu", 0), 0))
            for c in range(8):
                for j in range(2):
                    DO("pe", lambda h, c=c, j=j, bufp=bufp: h.matmul(
                        PA[:, j * 512:(j + 1) * 512], wg_buf[bufp][:, c, :],
                        h_fm[:, c, j * 512:(j + 1) * 512],
                        start=(c == 0), stop=(c == 7)), "pe")
            mg = pl.c["pe"]
            gu_used[("g", e, i)] = mg
            W("pe", "dve", gu_used.get(("prev_mult", 0), 0))
            for c in range(8):
                for j in range(2):
                    DO("pe", lambda h, c=c, j=j, bufp=bufp: h.matmul(
                        PB[:, j * 512:(j + 1) * 512], wu_buf[bufp][:, c, :],
                        h_fm[:, c, j * 512:(j + 1) * 512],
                        start=(c == 0), stop=(c == 7)), "pe")
            mu = pl.c["pe"]
            gu_used[("u", e, i)] = mu
            W("act", "pe", mg)
            W("act", "act", gu_used.get(("prev_silu", 0), 0))
            if act_gate_fn is not None:
                W("act", "dve", act_gate_fn(i))
            sl = DO("act", lambda h, i=i: h.activation(
                act_v[:, i, :], PA[:], AF.Silu), "act")
            gu_used[("prev_silu", 0)] = sl
            W("dve", "pe", mu)
            W("dve", "act", sl)
            W("dve", "dve", gu_used.get(("prev_mult", 0), 0))
            mm_ = DO("dve", lambda h, i=i: h.tensor_tensor(
                act_v[:, i, :], PB[:], act_v[:, i, :], ALU.mult), "dve")
            gu_used[("prev_mult", 0)] = mm_
            gu_used[("mult", e, i)] = mm_
            if ins is not None:
                ins(i)
        W("pe", "dm_wd", wdma[("wd", e)])
        W("pe", "dve", gu_used[("mult", e, 15)])
        if predown_fn is not None:
            W("pe", "dve", predown_fn())
        for mo in range(8):
            W("pe", "dve", ydone.get((e, mo - 1), 0))
            for i in range(16):
                for j in range(2):
                    DO("pe", lambda h, mo=mo, i=i, j=j: h.matmul(
                        PC[:, j * 512:(j + 1) * 512], wd_v[:, i, mo * 128:(mo + 1) * 128],
                        act_v[:, i, j * 512:(j + 1) * 512],
                        start=(i == 0), stop=(i == 15)), "pe")
            mdown = pl.c["pe"]
            W("dve", "pe", mdown)
            ydone[(e, mo)] = DO("dve", lambda h, mo=mo: h.tensor_tensor(
                F_v[:, mo, :], PC[:], F_v[:, mo, :], ALU.add), "dve")
            if ins_down is not None:
                ins_down(mo)
        wdma[("shwd_cons",)] = pl.c["pe"]
        return ydone[(e, 7)]

    if p3_only:
        x1_spill = {}
    have_p25 = not p3_only
    sc25h = [RW[:, 12288:14336].bitcast(F32), p25b]
    agh = [aggA, agg2b]
    rth = [rtA, rt1b]
    acts25h = {}
    dve25h = {}

    def p25h1_front_dve(t):
        par = t % 2
        W("sync", "dve", x1_spill.get(("cons", t - 2), 0))
        W("sync", "pe", wbuf_used.get(("lgpe", 7), 0))
        mld = DO("sync", lambda h, t=t, par=par: h.dma_start(
            p25st[par], x_d[t * 128:(t + 1) * 128, :]), f"dm_x{par}", 16)
        W("dve", f"dm_x{par}", mld)
        W("dve", "dm_x1w" if par == 0 else "dm_x1w1", x1_spill.get(("dma", t - 2), 0))
        DO("dve", lambda h, t=t, par=par: h.tensor_scalar(
            sc25h[par][:], p25st[par], stats_all[:, 2 * t:2 * t + 1],
            r_all[:, t:t + 1], ALU.subtract, ALU.mult), "dve")
        m = pl.c["dve"]; W("dve", "dve", m)
        DO("dve", lambda h, par=par: h.tensor_mul(sc25h[par][:], sc25h[par][:], sb_s[:]), "dve")
        m = pl.c["dve"]; W("dve", "dve", m)
        DO("dve", lambda h, par=par: h.tensor_add(sc25h[par][:], sc25h[par][:], sb_b[:]), "dve")
        m = pl.c["dve"]; W("dve", "dve", m)
        DO("dve", lambda h, t=t, par=par: h.tensor_add(
            sc25h[par][:], sc25h[par][:], attn_tm[:, t, :]), "dve")
        m = pl.c["dve"]; W("dve", "dve", m)
        DO("dve", lambda h, par=par: h.bn_stats(bn6[:, 0:6], sc25h[par][:, 0:512]), "dve")
        DO("dve", lambda h, par=par: h.bn_stats(bn6[:, 6:12], sc25h[par][:, 512:1024]), "dve")
        m = pl.c["dve"]; W("dve", "dve", m)
        DO("dve", lambda h, par=par: h.bn_aggr(agh[par][:], bn6[:]), "dve")
        m = pl.c["dve"]; W("dve", "dve", m)
        DO("dve", lambda h, par=par: h.tensor_scalar(
            rth[par][:], agh[par][:, 1:2], 1.0, LN_E1, ALU.mult, ALU.add), "dve")
        return pl.c["dve"]

    def p25h1_front_act(t, dcnt):
        par = t % 2
        W("act", "dve", dcnt)
        W("act", "act", pl.c["act"])
        DO("act", lambda h, par=par: h.activation(rt2[:], rth[par][:], AF.Ln), "act")
        m = pl.c["act"]; W("act", "act", m)
        DO("act", lambda h, par=par: h.activation(
            p25r[:, par:par + 1], rt2[:], AF.Exp, scale=-0.5), "act")
        return pl.c["act"]

    def p25h1_back(t, acnt):
        par = t % 2
        W("dve", "act", acnt)
        DO("dve", lambda h, par=par: h.tensor_scalar(
            sc25h[par][:], sc25h[par][:], agh[par][:, 0:1],
            p25r[:, par:par + 1], ALU.subtract, ALU.mult), "dve")
        m = pl.c["dve"]; W("dve", "dve", m)
        DO("dve", lambda h, par=par: h.tensor_add(sc25h[par][:], sc25h[par][:], p25st[par]), "dve")
        x1_spill[("cons", t)] = pl.c["dve"]
        m = pl.c["dve"]
        W("sync", "dve", m)
        x1_spill[("dma", t)] = DO("sync", lambda h, t=t, par=par: h.dma_start(
            x1_dram[t, :, :], sc25h[par][:]), "dm_x1w" if par == 0 else "dm_x1w1", 16)

    def p25h1_stage(p):
        if 0 <= p - 2 <= 7:
            p25h1_back(8 + p - 2, acts25h[8 + p - 2])
        if 0 <= p - 1 <= 7:
            acts25h[8 + p - 1] = p25h1_front_act(8 + p - 1, dve25h[8 + p - 1])
        if p <= 7:
            dve25h[8 + p] = p25h1_front_dve(8 + p)

    pl.barrier()
    emit_3a(0)

    def ins_sh0(i):
        if have_p25 and i <= 9:
            p25h1_stage(i)

    mF = emit_shared(0, ins=ins_sh0,
                     act_gate_fn=lambda i: dve25h.get(i, 0))
    W("sync", "dve", mF)
    mFsp = DO("sync", lambda h: h.dma_start(F_dram[0], RQ[:].bitcast(F32)), "dm_x1w", 16)
    emit_3a(1)
    moh = pl.c["dve"]
    msig = pl.c["act"]
    spst = {}
    scr2bf2 = [scr2.bitcast(BF16)[:, 0:1024], scr2.bitcast(BF16)[:, 1024:2048]]
    mscan_b = [0]
    pfx_fm8b = RW[:, 16384:20480].bitcast(F32)[0:8, :]

    def idx_oh(t):
        key = "a" if t % 2 == 0 else "b"
        ptt = PTa if t % 2 == 0 else PTb
        W("pe", "dve", max(ptg[key], moh))
        DO("pe", lambda h, ptt=ptt, t=t: h.transpose(
            ptt[0:8, 0:128], oh_all[:, t * E:(t + 1) * E], id_f[:]), "pe")
        mpe = pl.c["pe"]
        W("dve", "pe", mpe)
        ptg[key] = DO("dve", lambda h, ptt=ptt, t=t: h.tensor_copy(
            oh_fm8[:, t * 128:(t + 1) * 128], ptt[0:8, 0:128]), "dve")

    def idx_scan():
        m = pl.c["dve"]; W("dve", "dve", m)
        DO("dve", lambda h: h.tensor_tensor_scan(
            pfx_fm8b[:], oh_fm8[:], oh_fm8[:], 0.0, ALU.add, ALU.bypass), "dve")
        mscan_b[0] = pl.c["dve"]

    def idx_slot(t):
        key = "a" if t % 2 == 0 else "b"
        ptt = PTa if t % 2 == 0 else PTb
        W("pe", "dve", max(ptg[key], mscan_b[0]))
        DO("pe", lambda h, ptt=ptt, t=t: h.transpose(
            ptt[:, 0:8], pfx_fm8b[:, t * 128:(t + 1) * 128], id_f[0:8, 0:8]), "pe")
        mpe = pl.c["pe"]
        W("dve", "pe", mpe)
        W("dve", "dve", pl.c["dve"])
        DO("dve", lambda h, ptt=ptt: h.tensor_copy(pfx8[:], ptt[:, 0:8]), "dve")
        ptg[key] = pl.c["dve"]
        m = pl.c["dve"]; W("dve", "dve", m)
        DO("dve", lambda h, t=t: h.tensor_tensor(
            tmp8[:], pfx8[:], oh_all[:, t * E:(t + 1) * E], ALU.mult), "dve")
        m = pl.c["dve"]; W("dve", "dve", m)
        DO("dve", lambda h: h.reduce_sum(slotf[:], tmp8[:], AX.X), "dve")
        m = pl.c["dve"]; W("dve", "dve", m)
        DO("dve", lambda h, t=t: h.tensor_tensor(
            tmp8[:], ebase_b[:], oh_all[:, t * E:(t + 1) * E], ALU.mult), "dve")
        m = pl.c["dve"]; W("dve", "dve", m)
        DO("dve", lambda h: h.reduce_sum(rden[:], tmp8[:], AX.X), "dve")
        m = pl.c["dve"]; W("dve", "dve", m)
        DO("dve", lambda h: h.tensor_add(slotf[:], slotf[:], rden[:]), "dve")
        m = pl.c["dve"]; W("dve", "dve", m)
        DO("dve", lambda h: h.tensor_scalar(
            slotf[:], slotf[:], -1.0, float(E * CAP - 1), ALU.add, ALU.min), "dve")
        m = pl.c["dve"]; W("dve", "dve", m)
        DO("dve", lambda h, t=t: h.tensor_copy(
            idx_all[:, t:t + 1], slotf[:]), "dve")

    def xs_scatter(t):
        sp2 = t % 2
        W("dve", "act", msig)
        W("dve", f"dm_sp{sp2}", spst.get(("dma", t - 2), 0))
        DO("dve", lambda h, t=t, sp2=sp2: h.tensor_scalar(
            scr2bf2[sp2][:], h_tm[:, t, :], score_all[:, t:t + 1], None, ALU.mult), "dve")
        mxs = pl.c["dve"]
        W("gpsimd", "dve", mxs)
        spst[("dma", t)] = DO("gpsimd", lambda h, t=t, sp2=sp2: h.indirect_dma_start(
            xs_cp.ap(), IndirectOffsetOnAxis(ap=idx_all[:, t:t + 1], axis=0),
            scr2bf2[sp2][:], None), f"dm_sp{sp2}", 16)

    def ins_sh1(i):
        if i < 8:
            idx_oh(2 * i)
            idx_oh(2 * i + 1)
            if i == 7:
                idx_scan()
        else:
            idx_slot(2 * (i - 8))
            idx_slot(2 * (i - 8) + 1)

    def ins_down1(mo):
        xs_scatter(2 * mo)
        xs_scatter(2 * mo + 1)

    mF = emit_shared(1, pre_memset=mFsp, ins=ins_sh1, ins_down=ins_down1)
    msp = {0: pl.c["dm_sp0"], 1: pl.c["dm_sp1"]}
    xtr_done = {}
    for e in range(E):
        for rnd in range(4):
            slot = rnd % 2
            W("gpsimd", "dm_sp0", msp[0])
            W("gpsimd", "dm_sp1", msp[1])
            if e == 0 and rnd < 2:
                W("gpsimd", "dm_x1w" if rnd == 0 else "dm_x1w1",
                  x1_spill.get(("dma", 14 + rnd), 0))
            W("gpsimd", "pe", xtr_done.get((e, rnd - 2), xtr_done.get((e - 1, rnd + 2), 0)))
            mxe = DO("gpsimd", lambda h, e=e, rnd=rnd, slot=slot: h.dma_start(
                xs_tm_e[:, slot, :],
                xs_cp[e * CAP + rnd * 128:e * CAP + (rnd + 1) * 128, :]), "dm_xe", 16)
            for c in range(8):
                key = "a" if c % 2 == 0 else "b"
                ptt = PTa if c % 2 == 0 else PTb
                W("pe", "dve", ptg[key])
                if c == 0:
                    W("pe", "dm_xe", mxe)
                DO("pe", lambda h, ptt=ptt, slot=slot, c=c: h.transpose(
                    ptt, xs_tm_e[:, slot, c * 128:(c + 1) * 128], id_f[:]), "pe")
                mpe = pl.c["pe"]
                W("dve", "pe", mpe)
                ptg[key] = DO("dve", lambda h, ptt=ptt, c=c, rnd=rnd: h.tensor_copy(
                    xs_fm_e[:, c, rnd * 128:(rnd + 1) * 128], ptt), "dve")
            xtr_done[(e, rnd)] = pl.c["pe"]
        mxtr = pl.c["dve"]
        W("pe", "dve", mxtr)
        for i in range(16):
            bufp = i % 4
            W("gpsimd", "pe", gu_used.get(("u", e, i - 4), 0))
            wdma[("g", e, i)] = DO("gpsimd", lambda h, e=e, i=i, bufp=bufp: h.dma_start(
                wg_buf[bufp], rtg_d[e].rearrange(
                    "(c p) m -> p c m", p=128)[:, :, i * 128:(i + 1) * 128]), f"dm_wg{bufp}", 16)
            wdma[("u", e, i)] = DO("gpsimd", lambda h, e=e, i=i, bufp=bufp: h.dma_start(
                wu_buf[bufp], rtu_d[e].rearrange(
                    "(c p) m -> p c m", p=128)[:, :, i * 128:(i + 1) * 128]), f"dm_wg{bufp}", 16)
            W("pe", f"dm_wg{bufp}", wdma[("u", e, i)])
            W("pe", "act", gu_used.get(("prev_silu", 0), 0))
            for c in range(8):
                DO("pe", lambda h, c=c, bufp=bufp: h.matmul(
                    PA[:, 0:512], wg_buf[bufp][:, c, :],
                    xs_fm_e[:, c, :],
                    start=(c == 0), stop=(c == 7)), "pe")
            mg = pl.c["pe"]
            gu_used[("g", e, i)] = mg
            W("pe", "dve", gu_used.get(("prev_mult", 0), 0))
            for c in range(8):
                DO("pe", lambda h, c=c, bufp=bufp: h.matmul(
                    PB[:, 0:512], wu_buf[bufp][:, c, :],
                    xs_fm_e[:, c, :],
                    start=(c == 0), stop=(c == 7)), "pe")
            mu = pl.c["pe"]
            gu_used[("u", e, i)] = mu
            W("act", "pe", mg)
            W("act", "pe", ydone.get(("wd_cons", e - 1), wdma.get(("shwd_cons",), 0)))
            W("act", "act", gu_used.get(("prev_silu", 0), 0))
            sl = DO("act", lambda h, i=i: h.activation(
                act_sp_v[:, i, :], PA[:, 0:512], AF.Silu), "act")
            gu_used[("prev_silu", 0)] = sl
            gu_used[("silu", e, i)] = sl
            W("dve", "pe", mu)
            W("dve", "act", sl)
            W("dve", "dve", gu_used.get(("prev_mult", 0), 0))
            mm_ = DO("dve", lambda h, i=i: h.tensor_tensor(
                act_sp_v[:, i, :], PB[:, 0:512], act_sp_v[:, i, :], ALU.mult), "dve")
            gu_used[("mult", e, i)] = mm_
            gu_used[("prev_mult", 0)] = mm_
        W("gpsimd", "pe", ydone.get(("wd_cons", e - 1), wdma.get(("shwd_cons",), 0)))
        wdma[("wd", e)] = DO("gpsimd", lambda h, e=e: h.dma_start(
            wd_v, rtd_d[e].rearrange("(i p) m -> p i m", p=128)), "dm_wd", 16)
        W("pe", "dm_wd", wdma[("wd", e)])
        W("pe", "dve", gu_used[("mult", e, 15)])
        for mo in range(8):
            W("pe", "dve", ydone.get((e, mo - 1), 0))
            for i in range(16):
                DO("pe", lambda h, mo=mo, i=i: h.matmul(
                    PC[:, 0:512], wd_v[:, i, mo * 128:(mo + 1) * 128],
                    act_sp_v[:, i, :],
                    start=(i == 0), stop=(i == 15)), "pe")
            mdown = pl.c["pe"]
            W("dve", "pe", mdown)
            W("dve", "dve", pl.c["dve"])
            DO("dve", lambda h: h.tensor_copy(scr1[:, 0:512], PC[:, 0:512]), "dve")
            mev = pl.c["dve"]
            ydone[(e, mo)] = mev
            for q in range(4):
                key = "a" if q % 2 == 0 else "b"
                ptt = PTa if q % 2 == 0 else PTb
                W("pe", "dve", max(ptg[key], mev))
                DO("pe", lambda h, ptt=ptt, q=q: h.transpose(
                    ptt, scr1[:, q * 128:(q + 1) * 128], id_f[:]), "pe")
                mpe = pl.c["pe"]
                W("dve", "pe", mpe)
                if mo == 0 and q == 0:
                    W("dve", "dm_ye", ydone.get(("ydma", e - 1), 0))
                ptg[key] = DO("dve", lambda h, ptt=ptt, q=q, mo=mo: h.tensor_copy(
                    y_tm_e[:, q, mo * 128:(mo + 1) * 128], ptt), "dve")
        m = pl.c["dve"]
        W("gpsimd", "dve", m)
        ydone[("ydma", e)] = DO("gpsimd", lambda h, e=e: h.dma_start(
            y_cp[e * CAP:(e + 1) * CAP, :].rearrange("(q p) m -> p q m", p=128),
            y_tm_e[:]), "dm_ye", 16)
        ydone[("wd_cons", e)] = pl.c["pe"]
    mydma_all = pl.c["dm_ye"]
    pl.barrier()
    scrEp = [scr1, xs_tm_e[:, 0, :]]
    ygb = [scr2bf2[0], scr2bf2[1], RW[:, 14336:15360], RW[:, 15360:16384]]
    ygs = {}
    mlds = {}

    def ep_gather(t):
        sp4 = t % 4
        W("gpsimd", "dm_ye", mydma_all)
        W("gpsimd", "dve", out_dma.get(("ycons", t - 4), 0))
        ygs[t] = DO("gpsimd", lambda h, t=t, sp4=sp4: h.indirect_dma_start(
            ygb[sp4][:], None,
            y_cp.ap(), IndirectOffsetOnAxis(ap=idx_all[:, t:t + 1], axis=0)),
            f"dm_yg{sp4}", 16)

    def ep_segA(t):
        par = t % 2
        half = t // 8
        W("sync", "dve", out_dma.get(("x1cons", t - 2), 0))
        mlds[t] = DO("sync", lambda h, t=t, par=par: h.dma_start(
            [st0, st1][par], x1_dram[t, :, :]), f"dm_x1r{par}", 16)
        mldF = 0
        if half == 0:
            W("sync", "pe", out_dma.get(("fcons", t - 2), 0))
            mldF = DO("sync", lambda h, t=t, par=par: h.dma_start(
                scrF[par], F_dram[0].rearrange("p (c n) -> p c n", c=8)[:, :, t * 128:(t + 1) * 128]),
                "dm_x1w" if par == 0 else "dm_x1w1", 16)
        W("dve", "dve", out_dma.get(("x1cons", t - 2), 0))
        W("dve", "out", out_dma.get(("o", t - 2), 0))
        for c in range(8):
            tps = PA if c % 2 == 0 else PB
            W("pe", "dve", g_ep[c % 2])
            if half == 0:
                W("pe", "dm_x1w" if par == 0 else "dm_x1w1", mldF)
            DO("pe", lambda h, tps=tps, c=c, t=t, half=half, par=par: h.transpose(
                tps[:, 0:128],
                (scrF[par][:, c, :] if half == 0
                 else F_v[:, c, (t - 8) * 128:(t - 7) * 128]), id_f[:]), "pe")
            mpe = pl.c["pe"]
            W("dve", "pe", mpe)
            g_ep[c % 2] = DO("dve", lambda h, tps=tps, c=c, par=par: h.tensor_copy(
                scrEp[par][:, c * 128:(c + 1) * 128], tps[:, 0:128]), "dve")
        out_dma[("fcons", t)] = pl.c["pe"]
        if t + 2 < NT:
            ep_gather(t + 2)
        sp4 = t % 4
        m = pl.c["dve"]; W("dve", "dve", m)
        W("dve", f"dm_yg{sp4}", ygs[t])
        DO("dve", lambda h, sp4=sp4, par=par: h.tensor_add(
            scrEp[par][:], scrEp[par][:], ygb[sp4][:]), "dve")
        out_dma[("ycons", t)] = pl.c["dve"]

    def ep_segB(t):
        par = t % 2
        m = pl.c["dve"]; W("dve", "dve", m)
        DO("dve", lambda h, par=par: h.bn_stats(bn6[:, 0:6], scrEp[par][:, 0:512]), "dve")
        DO("dve", lambda h, par=par: h.bn_stats(bn6[:, 6:12], scrEp[par][:, 512:1024]), "dve")
        m = pl.c["dve"]; W("dve", "dve", m)
        DO("dve", lambda h: h.bn_aggr(agg2[:], bn6[:]), "dve")
        m = pl.c["dve"]; W("dve", "dve", m)
        DO("dve", lambda h: h.tensor_scalar(rt1[:], agg2[:, 1:2], 1.0, LN_E1, ALU.mult, ALU.add), "dve")
        m = pl.c["dve"]; W("act", "dve", m)
        W("act", "act", pl.c["act"])
        DO("act", lambda h: h.activation(rt2[:], rt1[:], AF.Ln), "act")
        m = pl.c["act"]; W("act", "act", m)
        DO("act", lambda h: h.activation(rt1[:], rt2[:], AF.Exp, scale=-0.5), "act")
        m = pl.c["act"]; W("dve", "act", m)
        DO("dve", lambda h, par=par: h.tensor_scalar(
            scrEp[par][:], scrEp[par][:], agg2[:, 0:1], rt1[:], ALU.subtract, ALU.mult), "dve")
        m = pl.c["dve"]; W("dve", "dve", m)
        W("dve", f"dm_x1r{par}", mlds[t])
        DO("dve", lambda h, par=par: h.tensor_add(
            scrEp[par][:], scrEp[par][:], [st0, st1][par]), "dve")
        out_dma[("x1cons", t)] = pl.c["dve"]
        m = pl.c["dve"]
        W("sync", "dve", m)
        out_dma[("o", t)] = DO("sync", lambda h, t=t, par=par: h.dma_start(
            out_d[t * 128:(t + 1) * 128, :], scrEp[par][:]), "out", 16)

    ep_gather(0)
    ep_gather(1)
    for t in range(NT + 1):
        if t < NT:
            ep_segA(t)
        if t >= 1:
            ep_segB(t - 1)

    W("sync", "out", pl.c["out"])

    return nc, pl, dict(locals())


def emit(nc, pl):
    with nc.Block() as block:
        @block.tensor
        def _(t):
            for f in pl.q["pe"]:
                f(t)

        @block.scalar
        def _(sc):
            for f in pl.q["act"]:
                f(sc)

        @block.vector
        def _(v):
            for f in pl.q["dve"]:
                f(v)

        @block.gpsimd
        def _(g):
            for f in pl.q["gpsimd"]:
                f(g)

        @block.sync
        def _(s):
            for f in pl.q["sync"]:
                f(s)
    return nc


_IDB = np.eye(128, dtype=ml_dtypes.bfloat16)
_IDF = np.eye(128, dtype=np.float32)


def make_in_maps(inputs):
    x = np.asarray(inputs["L0_attn_x"], dtype=np.float32)
    common = {
        "ln_scale": np.ascontiguousarray(inputs["L0_attn_ln_scale"], dtype=np.float32),
        "ln_bias": np.ascontiguousarray(inputs["L0_attn_ln_bias"], dtype=np.float32),
        "wq": np.ascontiguousarray(inputs["L0_attn_wq"], dtype=np.float32),
        "wk": np.ascontiguousarray(inputs["L0_attn_wk"], dtype=np.float32),
        "wv": np.ascontiguousarray(inputs["L0_attn_wv"], dtype=np.float32),
        "gate_w": np.ascontiguousarray(inputs["L0_ffn_gate_w"], dtype=np.float32),
        "sh_gate": np.ascontiguousarray(inputs["L0_ffn_sh_gate"], dtype=np.float32),
        "sh_up": np.ascontiguousarray(inputs["L0_ffn_sh_up"], dtype=np.float32),
        "sh_down": np.ascontiguousarray(inputs["L0_ffn_sh_down"], dtype=np.float32),
        "rt_gate": np.ascontiguousarray(inputs["L0_ffn_rt_gate"], dtype=np.float32),
        "rt_up": np.ascontiguousarray(inputs["L0_ffn_rt_up"], dtype=np.float32),
        "rt_down": np.ascontiguousarray(inputs["L0_ffn_rt_down"], dtype=np.float32),
        "ident_bf": _IDB,
        "ident_f32": _IDF,
        "iota128": np.arange(128, dtype=np.float32).reshape(128, 1),
        "ebase": (np.arange(E) * CAP).astype(np.float32),
    }
    in_maps = []
    for c in range(NC_N):
        b, tb = c // 4, c % 4
        xb = x[b]
        xr = np.ascontiguousarray(np.concatenate([xb[tb * TC:], xb[:tb * TC]], axis=0))
        m = dict(common)
        m["x"] = xr
        in_maps.append(m)
    return in_maps


def kernel(**inputs):
    nc, pl, _h = build_program()
    emit(nc, pl)
    in_maps = make_in_maps(inputs)
    res = run_bass_kernel_spmd(nc, in_maps, core_ids=list(range(NC_N)))
    full = np.empty((B, S, D), np.float32)
    for c in range(NC_N):
        b, tb = c // 4, c % 4
        full[b, tb * TC:(tb + 1) * TC] = np.asarray(res.results[c]["out"])
    return full


# version 29
# speedup vs baseline: 1.3451x; 1.3451x over previous
import numpy as np
import ml_dtypes
import concourse.bass as bass
import concourse.mybir as mybir
from concourse.bass_utils import run_bass_kernel_spmd
from concourse.bass import IndirectOffsetOnAxis

F32 = mybir.dt.float32
BF16 = mybir.dt.bfloat16
AF = mybir.ActivationFunctionType
ALU = mybir.AluOpType
AX = mybir.AxisListType

B, S, D, H, HD, I, E = 2, 8192, 1024, 16, 64, 2048, 8
NC_N = 8
TC = 2048
NT = 16
NSEC = 8
PAIRS = 8
NKC = 64
ATT_SCALE = 1.0 / 32.0
CSHIFT = 10.0
LN_E1 = 1e-5
COMB_A = 1.0 + 1e-6
COMB_B = 1e-11
CAP = 512


class Pl:

    def __init__(self, nc):
        self.nc = nc
        names = ["pe", "act", "dve", "dm_const", "dm_w",
                 "dm_x0", "dm_x1", "dm_k0", "dm_k1", "dm_v",
                 "dm_kv0", "dm_kv1", "dm_x1w", "dm_x1w1", "dm_x1r0", "dm_x1r1",
                 "dm_wg0", "dm_wg1", "dm_wg2", "dm_wg3", "dm_wd", "dm_z",
                 "dm_sp0", "dm_sp1", "dm_xe", "dm_ye", "dm_yg0", "dm_yg1",
                 "dm_yg2", "dm_yg3", "out"]
        self.sem = {n: nc.alloc_semaphore("sem_" + n) for n in names}
        self.c = {n: 0 for n in names}
        self.q = {"pe": [], "act": [], "dve": [], "gpsimd": [], "sync": []}

    def w(self, eng, sem, val):
        if val is None or val <= 0:
            return
        s = self.sem[sem]
        self.q[eng].append(lambda h, s=s, v=val: h.wait_ge(s, v))

    def do(self, eng, fn, sem, k=1):
        self.c[sem] += k
        s, kk = self.sem[sem], k
        self.q[eng].append(lambda h, fn=fn, s=s, kk=kk: fn(h).then_inc(s, kk))
        return self.c[sem]

    def do0(self, eng, fn):
        self.q[eng].append(fn)

    def snap(self):
        return dict(self.c)

    def barrier(self, engs=("pe", "act", "dve", "gpsimd", "sync")):
        sn = self.snap()
        for e in engs:
            for s in self.sem:
                self.w(e, s, sn[s])
        return sn


def build_program(upto=99, experts=None, p3_only=False):
    nc = bass.Bass("TRN2", target_bir_lowering=False)

    x_d = nc.dram_tensor("x", [S, D], F32, kind="ExternalInput")
    lns_d = nc.dram_tensor("ln_scale", [D], F32, kind="ExternalInput")
    lnb_d = nc.dram_tensor("ln_bias", [D], F32, kind="ExternalInput")
    wq_d = nc.dram_tensor("wq", [D, D], F32, kind="ExternalInput")
    wk_d = nc.dram_tensor("wk", [D, D], F32, kind="ExternalInput")
    wv_d = nc.dram_tensor("wv", [D, D], F32, kind="ExternalInput")
    gw_d = nc.dram_tensor("gate_w", [D, E], F32, kind="ExternalInput")
    shg_d = nc.dram_tensor("sh_gate", [D, I], F32, kind="ExternalInput")
    shu_d = nc.dram_tensor("sh_up", [D, I], F32, kind="ExternalInput")
    shd_d = nc.dram_tensor("sh_down", [I, D], F32, kind="ExternalInput")
    rtg_d = nc.dram_tensor("rt_gate", [E, D, I], F32, kind="ExternalInput")
    rtu_d = nc.dram_tensor("rt_up", [E, D, I], F32, kind="ExternalInput")
    rtd_d = nc.dram_tensor("rt_down", [E, I, D], F32, kind="ExternalInput")
    idb_d = nc.dram_tensor("ident_bf", [128, 128], BF16, kind="ExternalInput")
    iota_d = nc.dram_tensor("iota128", [128, 1], F32, kind="ExternalInput")
    ebase_d = nc.dram_tensor("ebase", [E], F32, kind="ExternalInput")
    idf_d = nc.dram_tensor("ident_f32", [128, 128], F32, kind="ExternalInput")

    out_d = nc.dram_tensor("out", [TC, D], F32, kind="ExternalOutput")

    k_dram = nc.dram_tensor("k_scr", [PAIRS, 128, S], BF16, kind="Internal")
    v_dram = nc.dram_tensor("v_scr", [PAIRS, 128, 2, NKC, 66], BF16, kind="Internal")
    x1_dram = nc.dram_tensor("x1_scr", [NT, 128, D], F32, kind="ExternalInput" if p3_only else "Internal")
    xs_cp = nc.dram_tensor("xs_cp", [E * CAP, D], BF16, kind="Internal")
    y_cp = nc.dram_tensor("y_cp", [E * CAP, D], BF16, kind="Internal")
    F_dram = nc.dram_tensor("F_scr", [1, 128, 8192], F32, kind="Internal")

    RW = nc.alloc_sbuf_tensor("RW", [128, 24576], BF16)
    RK = nc.alloc_sbuf_tensor("RK", [128, 16384], BF16)
    RV = nc.alloc_sbuf_tensor("RV", [128, 17408], BF16)
    RQ = nc.alloc_sbuf_tensor("RQ", [128, 16384], BF16)
    RT = nc.alloc_sbuf_tensor("RT", [128, 16384], BF16)
    RS = nc.alloc_sbuf_tensor("RS", [128, 4096], F32)
    sb_s = nc.alloc_sbuf_tensor("sb_s", [128, D], F32)
    sb_b = nc.alloc_sbuf_tensor("sb_b", [128, D], F32)
    stats_all = nc.alloc_sbuf_tensor("stats_all", [128, 2 * NSEC * 8], F32)
    r_all = nc.alloc_sbuf_tensor("r_all", [128, NSEC * 8], F32)
    id_bf = nc.alloc_sbuf_tensor("id_bf", [128, 128], BF16)
    id_f = nc.alloc_sbuf_tensor("id_f", [128, 128], F32)
    gate_f = nc.alloc_sbuf_tensor("gate_f", [128, 8 * E], F32)
    gate_hi = nc.alloc_sbuf_tensor("gate_hi", [128, 8 * E], BF16)
    gate_lo = nc.alloc_sbuf_tensor("gate_lo", [128, 8 * E], BF16)
    shift_t = nc.alloc_sbuf_tensor("shift_t", [128, 1], F32)
    se_all = nc.alloc_sbuf_tensor("se_all", [128, NT * E], F32)
    bn6 = nc.alloc_sbuf_tensor("bn6", [128, 12], F32)
    agg2 = nc.alloc_sbuf_tensor("agg2", [128, 2], F32)
    rt1 = nc.alloc_sbuf_tensor("rt1", [128, 1], F32)
    rt2 = nc.alloc_sbuf_tensor("rt2", [128, 1], F32)
    rt1b = nc.alloc_sbuf_tensor("rt1b", [128, 1], F32)
    agg2b = nc.alloc_sbuf_tensor("agg2b", [128, 2], F32)
    p25r = nc.alloc_sbuf_tensor("p25r", [128, 2], F32)
    aggA = nc.alloc_sbuf_tensor("aggA", [128, 2], F32)
    rtA = nc.alloc_sbuf_tensor("rtA", [128, 1], F32)
    lsb = nc.alloc_sbuf_tensor("lsb", [128, E], F32)
    tvb = nc.alloc_sbuf_tensor("tvb", [128, 1], F32)
    ohb = nc.alloc_sbuf_tensor("ohb", [128, E], F32)
    scb = nc.alloc_sbuf_tensor("scb", [128, 1], F32)
    rden = nc.alloc_sbuf_tensor("rden", [128, 1], F32)
    iota_sb = nc.alloc_sbuf_tensor("iota_sb", [128, 1], F32)
    ebase_b = nc.alloc_sbuf_tensor("ebase_b", [128, E], F32)
    oh_all = nc.alloc_sbuf_tensor("oh_all", [128, NT * E], F32)
    score_all = nc.alloc_sbuf_tensor("score_all", [128, NT], F32)
    idx_all = nc.alloc_sbuf_tensor("idx_all", [128, NT], mybir.dt.int32)
    pfx8 = nc.alloc_sbuf_tensor("pfx8", [128, E], F32)
    tmp8 = nc.alloc_sbuf_tensor("tmp8", [128, E], F32)
    slotf = nc.alloc_sbuf_tensor("slotf", [128, 1], F32)


    PA = nc.alloc_psum_tensor("PA", [128, 1024], F32)
    PB = nc.alloc_psum_tensor("PB", [128, 1024], F32)
    PC = nc.alloc_psum_tensor("PC", [128, 1024], F32)
    PD = nc.alloc_psum_tensor("PD", [128, 1024], F32)
    PTa = PD[:, 0:128]
    PTb = PD[:, 512:640]

    pl = Pl(nc)
    W, DO, DO0 = pl.w, pl.do, pl.do0
    ptg = {"a": 0, "b": 0}
    g_kq = {0: 0, 1: 0}
    g_v = {0: 0, 1: 0, 2: 0, 3: 0}
    g_ep = {0: 0, 1: 0}

    wq_v = RW[:, 0:8192].rearrange("p (c m) -> p c m", c=8)
    wk_v = RW[:, 8192:16384].rearrange("p (c m) -> p c m", c=8)
    wv_v = RW[:, 16384:24576].rearrange("p (c m) -> p c m", c=8)
    xnfm = [RK[:, 0:8192].rearrange("p (c n) -> p c n", c=8),
            RK[:, 8192:16384].rearrange("p (c n) -> p c n", c=8)]
    rv32 = RV[:, 0:8192].bitcast(F32)
    xstage = [rv32[:, 0:1024], rv32[:, 1024:2048]]
    xnf32 = rv32[:, 2048:3072]
    xnf32b = rv32[:, 3072:4096]
    kevac = [RV[:, 12288:13312], RV[:, 13312:14336]]
    vstage = RV[:, 14336:16448].rearrange("p (h t n) -> p h t n", h=4, t=8)
    q_fm = RQ[:].rearrange("p (r n) -> p r n", r=PAIRS)
    kbuf = [RK[:, 0:8192], RK[:, 8192:16384]]
    vbuf = [RV[:, 0:8448].rearrange("p (h t n) -> p h t n", h=2, t=NKC),
            RV[:, 8448:16896].rearrange("p (h t n) -> p h t n", h=2, t=NKC)]
    probsA = [RS[:, 0:512].bitcast(BF16), RS[:, 512:1024].bitcast(BF16)]
    probsB = [RS[:, 1024:1536].bitcast(BF16), RS[:, 1536:2048].bitcast(BF16)]
    pvs = RW[:, 12288:14336].bitcast(F32)[0:65, :]
    p25b = RW[:, 14336:16384].bitcast(F32)
    attn_tm = RT[:].rearrange("p (t n) -> p t n", t=NT)
    st0 = RS[:, 0:1024]
    st1 = RS[:, 1024:2048]
    scr1 = RS[:, 2048:3072]
    scr2 = RS[:, 3072:4096]
    scr2bf = RS[:, 3072:3584].bitcast(BF16)
    h_tm = RK[:].rearrange("p (t n) -> p t n", t=NT)
    h_fm = RW[:, 0:8192].rearrange("p (c n) -> p c n", c=8)
    wg_buf = [RW[:, 8192:9216].rearrange("p (c m) -> p c m", c=8),
              RW[:, 9216:10240].rearrange("p (c m) -> p c m", c=8),
              RW[:, 20480:21504].rearrange("p (c m) -> p c m", c=8),
              RW[:, 21504:22528].rearrange("p (c m) -> p c m", c=8)]
    wu_buf = [RW[:, 10240:11264].rearrange("p (c m) -> p c m", c=8),
              RW[:, 11264:12288].rearrange("p (c m) -> p c m", c=8),
              RW[:, 22528:23552].rearrange("p (c m) -> p c m", c=8),
              RW[:, 23552:24576].rearrange("p (c m) -> p c m", c=8)]
    xs_fm = RW[:, 12288:20480].rearrange("p (c n) -> p c n", c=8)
    lo_fm = xs_fm
    wd_v = RV[:, 0:16384].rearrange("p (i m) -> p i m", i=16)
    xs_tm_e = RW[:, 12288:16384].bitcast(F32).rearrange("p (q m) -> p q m", q=2)
    xs_fm_e = RW[:, 16384:20480].rearrange("p (c n) -> p c n", c=8)
    y_tm_e = RK[:, 0:4096].rearrange("p (q m) -> p q m", q=4)
    scrF = [rv32[:, 0:1024].rearrange("p (c n) -> p c n", c=8),
            rv32[:, 1024:2048].rearrange("p (c n) -> p c n", c=8)]
    F_v = RQ[:].bitcast(F32).rearrange("p (c n) -> p c n", c=8)
    act_v = RT[:].rearrange("p (i n) -> p i n", i=16)
    act_sp_v = RT[:, 0:8192].rearrange("p (i n) -> p i n", i=16)
    oh_fm8 = RS[0:8, 0:2048]
    pfx_fm8 = RS[0:8, 2048:4096]

    DO("sync", lambda h: h.dma_start(sb_s[:], lns_d.ap().unsqueeze(0).partition_broadcast(128)), "dm_const", 16)
    DO("sync", lambda h: h.dma_start(sb_b[:], lnb_d.ap().unsqueeze(0).partition_broadcast(128)), "dm_const", 16)
    DO("sync", lambda h: h.dma_start(id_bf[:], idb_d[:]), "dm_const", 16)
    DO("sync", lambda h: h.dma_start(id_f[:], idf_d[:]), "dm_const", 16)
    DO("sync", lambda h: h.dma_start(
        gate_f[:].rearrange("p (c e) -> p c e", c=8),
        gw_d.ap().rearrange("(c p) e -> p c e", p=128)), "dm_const", 16)
    DO("sync", lambda h: h.dma_start(iota_sb[:], iota_d[:]), "dm_const", 16)
    DO("sync", lambda h: h.dma_start(ebase_b[:], ebase_d.ap().unsqueeze(0).partition_broadcast(128)), "dm_const", 16)
    DO("gpsimd", lambda h: h.dma_start(wq_v, wq_d.ap().rearrange("(c p) m -> p c m", p=128)), "dm_w", 16)
    DO("gpsimd", lambda h: h.dma_start(wk_v, wk_d.ap().rearrange("(c p) m -> p c m", p=128)), "dm_w", 16)
    DO("gpsimd", lambda h: h.dma_start(wv_v, wv_d.ap().rearrange("(c p) m -> p c m", p=128)), "dm_w", 16)
    m_const_dma = pl.c["dm_const"]
    m_w_dma = pl.c["dm_w"]

    W("dve", "dm_const", m_const_dma)
    W("dve", "dm_w", m_w_dma)
    DO("dve", lambda h: h.memset(shift_t[:], -CSHIFT), "dve")
    DO("dve", lambda h: h.tensor_copy(gate_hi[:], gate_f[:]), "dve")
    m = pl.c["dve"]
    W("dve", "dve", m)
    DO("dve", lambda h: h.tensor_sub(gate_lo[:], gate_f[:], gate_hi[:]), "dve")
    DO("dve", lambda h: h.memset(vstage[:, :, :, 64:66], 0.0), "dve")
    m = pl.c["dve"]
    W("dve", "dve", m)
    DO("dve", lambda h: h.memset(vstage[:, :, :, 64:65], 1.0), "dve")
    m_init_dve = pl.c["dve"]
    DO("dve", lambda h: h.memset(kevac[0], 0.0), "dve")
    mz = pl.c["dve"]
    W("gpsimd", "dve", mz)
    for rr in range(E * CAP // 128):
        DO("gpsimd", lambda h, rr=rr: h.dma_start(
            xs_cp[rr * 128:(rr + 1) * 128, :], kevac[0]), "dm_z", 16)
    m_zfill = pl.c["dm_z"]

    if upto < 1:
        return nc, pl, dict(locals())

    if not p3_only:
        xn_done_dve = [0] * NSEC
        proj_done_pe = [0] * NSEC
        load_cnt = {}
        tr_evac_dve = {}

        def p1_load(sec, t, par):
            g = sec * 8 + t
            W("sync", "dve", load_cnt.get(("cons", g - 2), 0))
            load_cnt[("ld", g)] = DO("sync", lambda h, g=g, par=par: h.dma_start(
                xstage[par], x_d[g * 128:(g + 1) * 128, :]), f"dm_x{par}", 16)

        def p1_stats(sec, t, par):
            g = sec * 8 + t
            W("dve", f"dm_x{par}", load_cnt[("ld", g)])
            DO("dve", lambda h, par=par: h.bn_stats(bn6[:, 0:6], xstage[par][:, 0:512]), "dve")
            DO("dve", lambda h, par=par: h.bn_stats(bn6[:, 6:12], xstage[par][:, 512:1024]), "dve")
            m = pl.c["dve"]
            W("dve", "dve", m)
            DO("dve", lambda h, g=g: h.bn_aggr(stats_all[:, 2 * g:2 * g + 2], bn6[:]), "dve")
            m = pl.c["dve"]
            W("dve", "dve", m)
            rt = rt1 if par == 0 else rt1b
            DO("dve", lambda h, g=g, rt=rt: h.tensor_scalar(
                rt[:], stats_all[:, 2 * g + 1:2 * g + 2], COMB_A, COMB_B, ALU.mult, ALU.add), "dve")
            m = pl.c["dve"]
            W("act", "dve", m)
            W("act", "act", pl.c["act"])
            DO("act", lambda h, rt=rt: h.activation(rt2[:], rt[:], AF.Ln), "act")
            m = pl.c["act"]
            W("act", "act", m)
            DO("act", lambda h, g=g: h.activation(
                r_all[:, g:g + 1], rt2[:], AF.Exp, scale=-0.5), "act")
            return pl.c["act"]

        def p1_xn(sec, t, par, r_ready_act):
            g = sec * 8 + t
            sp = sec % 2
            if t == 0 and sec >= 2:
                W("dve", "pe", proj_done_pe[sec - 2])
            W("dve", "act", r_ready_act)
            W("dve", "pe", tr_evac_dve.get(("xnb", g - 1), 0))
            DO("dve", lambda h, par=par, g=g: h.tensor_scalar(
                xnf32[:], xstage[par], stats_all[:, 2 * g:2 * g + 1],
                r_all[:, g:g + 1], ALU.subtract, ALU.mult), "dve")
            load_cnt[("cons", g)] = pl.c["dve"]
            m = pl.c["dve"]
            W("dve", "dve", m)
            DO("dve", lambda h: h.tensor_mul(xnf32[:], xnf32[:], sb_s[:]), "dve")
            m = pl.c["dve"]
            W("dve", "dve", m)
            DO("dve", lambda h: h.tensor_add(xnf32b[:], xnf32[:], sb_b[:]), "dve")
            m_xnbf = pl.c["dve"]
            tr_evac_dve[("xnb", g)] = None
            for c in range(8):
                key = "a" if c % 2 == 0 else "b"
                ptt = PTa if c % 2 == 0 else PTb
                W("pe", "dve", m_xnbf)
                W("pe", "dve", ptg[key])
                DO("pe", lambda h, ptt=ptt, c=c: h.transpose(
                    ptt, xnf32b[:, c * 128:(c + 1) * 128], id_f[:]), "pe")
                mpe = pl.c["pe"]
                W("dve", "pe", mpe)
                ptg[key] = DO("dve", lambda h, sp=sp, c=c, t=t, ptt=ptt: h.tensor_copy(
                    xnfm[sp][:, c, t * 128:(t + 1) * 128], ptt), "dve")
            tr_evac_dve[("xnb", g)] = pl.c["pe"]
            return pl.c["dve"]

        kq_psum = [PA, PB]
        v_psum = [PC[:, 0:256], PC[:, 512:768], PB[:, 0:256], PB[:, 512:768]]
        kdma_last = {0: 0, 1: 0}
        vdma_last = [0]
        W("pe", "dm_w", m_w_dma)
        W("pe", "dm_const", m_const_dma)

        kv_evac = {}

        def wait_g(eng, gk):
            if isinstance(gk, tuple):
                W(eng, gk[0], gk[1])
            else:
                W(eng, "dve", gk)

        def proj_chunk(psec, i):
            spp = psec % 2
            ps = kq_psum[i % 2]
            W("pe", "dve", xn_done_dve[psec])
            wait_g("pe", g_kq[i % 2])
            for c in range(8):
                for j in range(2):
                    DO("pe", lambda h, ps=ps, c=c, j=j, i=i, spp=spp: h.matmul(
                        ps[:, j * 512:(j + 1) * 512],
                        wk_v[:, c, i * 128:(i + 1) * 128],
                        xnfm[spp][:, c, j * 512:(j + 1) * 512],
                        start=(c == 0), stop=(c == 7)), "pe")
            mpe = pl.c["pe"]
            if psec == 0 and i == 0:
                W("act", "dm_z", m_zfill)
            W("act", "pe", mpe)
            W("act", f"dm_k{i % 2}", kdma_last[i % 2])
            kv_evac[i] = DO("act", lambda h, ps=ps: h.copy(
                kevac[i % 2], ps[:]), "act")
            g_kq[i % 2] = ("act", kv_evac[i])
            W("gpsimd", "act", kv_evac[i])
            load_cnt[("kdma", i)] = DO("gpsimd", lambda h, i=i, psec=psec: h.dma_start(
                k_dram[i, :, psec * 1024:(psec + 1) * 1024], kevac[i % 2]), f"dm_k{i % 2}", 16)
            kdma_last[i % 2] = load_cnt[("kdma", i)]
            grp, thalf = i // 2, (i % 2) * 4
            for t in range(thalf, thalf + 4):
                ps = v_psum[t % 4]
                if t % 4 >= 2:
                    wait_g("pe", g_kq[1])
                wait_g("pe", g_v[t % 4])
                for c in range(8):
                    DO("pe", lambda h, ps=ps, c=c, grp=grp, t=t, spp=spp: h.matmul(
                        ps, xnfm[spp][:, c, t * 128:(t + 1) * 128],
                        wv_v[:, c, grp * 256:(grp + 1) * 256],
                        start=(c == 0), stop=(c == 7)), "pe")
                mpe = pl.c["pe"]
                W("act", "pe", mpe)
                W("act", "dm_v", vdma_last[0] if t <= 1 else 0)
                kv_evac[("v", grp, t)] = DO("act", lambda h, ps=ps, t=t: h.copy(
                    vstage[:, :, t, 0:64],
                    ps.rearrange("p (hh n) -> p hh n", hh=4)), "act")
                g_v[t % 4] = ("act", kv_evac[("v", grp, t)])
                if t % 4 >= 2:
                    g_kq[1] = ("act", kv_evac[("v", grp, t)])
            if i % 2 == 1:
                W("gpsimd", "act", kv_evac[("v", grp, 7)])
                for qq in range(2):
                    vdma_last[0] = DO("gpsimd", lambda h, grp=grp, psec=psec, qq=qq: h.dma_start(
                        v_dram[2 * grp + qq, :, :, psec * 8:psec * 8 + 8, :],
                        vstage[:, 2 * qq:2 * qq + 2, :, :]), "dm_v", 16)
            if psec < 2:
                ps = kq_psum[i % 2]
                wait_g("pe", g_kq[i % 2])
                for c in range(8):
                    for j in range(2):
                        DO("pe", lambda h, ps=ps, c=c, j=j, i=i, spp=spp: h.matmul(
                            ps[:, j * 512:(j + 1) * 512],
                            wq_v[:, c, i * 128:(i + 1) * 128],
                            xnfm[spp][:, c, j * 512:(j + 1) * 512],
                            start=(c == 0), stop=(c == 7)), "pe")
                mpe = pl.c["pe"]
                W("act", "pe", mpe)
                kv_evac[("q", i)] = DO("act", lambda h, ps=ps, i=i, psec=psec: h.copy(
                    q_fm[:, i, psec * 1024:(psec + 1) * 1024], ps[:]), "act")
                g_kq[i % 2] = ("act", kv_evac[("q", i)])

        for sec in range(NSEC):
            sp = sec % 2
            r_acts = {}
            for t in range(8):
                par = t % 2
                p1_load(sec, t, par)
                r_acts[t] = p1_stats(sec, t, par)
                if t >= 1:
                    p1_xn(sec, t - 1, (t - 1) % 2, r_acts[t - 1])
                if sec >= 1:
                    proj_chunk(sec - 1, t)
            xn_done_dve[sec] = p1_xn(sec, 7, 1, r_acts[7])
            if sec >= 1:
                proj_done_pe[sec - 1] = pl.c["pe"]
        for i in range(8):
            proj_chunk(NSEC - 1, i)
        proj_done_pe[NSEC - 1] = pl.c["pe"]

        sn1 = pl.barrier()

        if upto < 2:
            return nc, pl, dict(locals())

        x1_spill = {}
        p25_gate = [0]
        p25st = [RW[:, 16384:18432].bitcast(F32), RW[:, 18432:20480].bitcast(F32)]
        scr2p = [scr2, p25b]
        aggp = [agg2, agg2b]
        rtp = [rt1, rt1b]

        def p25_front_dve(t):
            par = t % 2
            W("sync", "dve", x1_spill.get(("cons", t - 2), 0))
            mld = DO("sync", lambda h, t=t, par=par: h.dma_start(
                p25st[par], x_d[t * 128:(t + 1) * 128, :]), f"dm_x{par}", 16)
            W("dve", f"dm_x{par}", mld)
            W("dve", "dve", p25_gate[0])
            DO("dve", lambda h, t=t, par=par: h.tensor_scalar(
                scr1[:], p25st[par], stats_all[:, 2 * t:2 * t + 1],
                r_all[:, t:t + 1], ALU.subtract, ALU.mult), "dve")
            m = pl.c["dve"]; W("dve", "dve", m)
            DO("dve", lambda h: h.tensor_mul(scr1[:], scr1[:], sb_s[:]), "dve")
            m = pl.c["dve"]; W("dve", "dve", m)
            DO("dve", lambda h: h.tensor_add(scr1[:], scr1[:], sb_b[:]), "dve")
            m = pl.c["dve"]; W("dve", "dve", m)
            W("dve", "dm_x1w" if par == 0 else "dm_x1w1", x1_spill.get(("dma", t - 2), 0))
            DO("dve", lambda h, t=t, par=par: h.tensor_add(scr2p[par][:], attn_tm[:, t, :], scr1[:]), "dve")
            m = pl.c["dve"]; W("dve", "dve", m)
            DO("dve", lambda h, par=par: h.bn_stats(bn6[:, 0:6], scr2p[par][:, 0:512]), "dve")
            DO("dve", lambda h, par=par: h.bn_stats(bn6[:, 6:12], scr2p[par][:, 512:1024]), "dve")
            m = pl.c["dve"]; W("dve", "dve", m)
            DO("dve", lambda h, par=par: h.bn_aggr(aggp[par][:], bn6[:]), "dve")
            m = pl.c["dve"]; W("dve", "dve", m)
            DO("dve", lambda h, par=par: h.tensor_scalar(
                rtp[par][:], aggp[par][:, 1:2], 1.0, LN_E1, ALU.mult, ALU.add), "dve")
            return pl.c["dve"]

        def p25_front_act(t, dve_cnt):
            par = t % 2
            W("act", "dve", dve_cnt)
            W("act", "act", pl.c["act"])
            DO("act", lambda h, par=par: h.activation(rt2[:], rtp[par][:], AF.Ln), "act")
            m = pl.c["act"]; W("act", "act", m)
            DO("act", lambda h, par=par: h.activation(
                p25r[:, par:par + 1], rt2[:], AF.Exp, scale=-0.5), "act")
            return pl.c["act"]

        def p25_front(t):
            return p25_front_act(t, p25_front_dve(t))

        def p25_back(t, act_cnt):
            par = t % 2
            W("dve", "act", act_cnt)
            DO("dve", lambda h, par=par: h.tensor_scalar(
                scr2p[par][:], scr2p[par][:], aggp[par][:, 0:1],
                p25r[:, par:par + 1], ALU.subtract, ALU.mult), "dve")
            m = pl.c["dve"]; W("dve", "dve", m)
            DO("dve", lambda h, par=par: h.tensor_add(scr2p[par][:], scr2p[par][:], p25st[par]), "dve")
            x1_spill[("cons", t)] = pl.c["dve"]
            m = pl.c["dve"]
            W("sync", "dve", m)
            x1_spill[("dma", t)] = DO("sync", lambda h, t=t, par=par: h.dma_start(
                x1_dram[t, :, :], scr2p[par][:]), "dm_x1w" if par == 0 else "dm_x1w1", 16)

        acts25 = {}
        dve25 = {}

        def emit_p25(half, attn_dve_done):
            p25_gate[0] = attn_dve_done
            ts_ = range(half * 8, half * 8 + 8)
            for t in ts_:
                acts25[t] = p25_front(t)
                if t > half * 8:
                    p25_back(t - 1, acts25[t - 1])
            p25_back(ts_[-1], acts25[ts_[-1]])

        def p25_stage(p):
            if 0 <= p - 2 <= 7:
                p25_back(p - 2, acts25[p - 2])
            if 0 <= p - 1 <= 7:
                acts25[p - 1] = p25_front_act(p - 1, dve25[p - 1])
            if p <= 7:
                dve25[p] = p25_front_dve(p)

        kv_load = {}

        def load_pair(idx):
            pair, par = idx % PAIRS, idx % 2
            W("sync", "pe", kv_load.get(("consumed", idx - 2), 0))
            DO("sync", lambda h, pair=pair, par=par: h.dma_start(
                kbuf[par], k_dram[pair, :, :]), f"dm_kv{par}", 16)
            kv_load[("ld", idx)] = DO("sync", lambda h, pair=pair, par=par: h.dma_start(
                vbuf[par], v_dram[pair, :, :, :, :]), f"dm_kv{par}", 16)

        load_pair(0)
        load_pair(1)

        pvm = {}
        prevexp = {"A": 0, "B": 0}
        prevpv = {("A", k): 0 for k in range(4)}
        prevpv.update({("B", k): 0 for k in range(4)})
        pv_ps = {"A": PC[0:65, :], "B": PD[0:65, :]}
        sc_ps = {"A": PA, "B": PB}
        probs = {"A": [probsA[0], probsA[1],
                       RW[:, 20480:21504], RW[:, 21504:22528]],
                 "B": [probsB[0], probsB[1],
                       RW[:, 22528:23552], RW[:, 23552:24576]]}
        pvsAB = RW[:, 8192:12288].bitcast(F32)[0:65, :]
        d_tps = [PC[:, 0:65], PC[:, 512:577], PD[:, 0:65], PD[:, 512:577]]
        pending = [None]
        dtr = {}
        last_tr_pe = [0]

        def defer_step(s):
            pd_ = pending[0]
            hh = 0 if s < 8 else 1
            t = s % 8
            tg = pd_["qh"] * 8 + t
            head = 2 * pd_["pair"] + hh
            tps = d_tps[s % 4]
            W("pe", "dve", pd_["gate"])
            W("pe", "dve", dtr.get(("ts", s - 4), 0))
            DO("pe", lambda h, tps=tps, hh=hh, t=t: h.transpose(
                tps,
                pvsAB[:, hh * 1024 + t * 128: hh * 1024 + (t + 1) * 128],
                id_f[0:65, 0:65]), "pe")
            last_tr_pe[0] = pl.c["pe"]
            mtr = pl.c["pe"]
            W("dve", "pe", mtr)
            W("dve", "dve", pl.c["dve"])
            DO("dve", lambda h, tps=tps: h.reciprocal(rden[:], tps[:, 64:65]), "dve")
            mr = pl.c["dve"]
            W("dve", "dve", mr)
            dtr[("ts", s)] = DO("dve", lambda h, tps=tps, tg=tg, head=head: h.tensor_scalar(
                attn_tm[:, tg, head * 64:(head + 1) * 64],
                tps[:, 0:64], rden[:], None, ALU.mult), "dve")

        def emit_pv(HH, kc, par, hh, start, stop):
            for j in range(2):
                DO("pe", lambda h, HH=HH, par=par, hh=hh, kc=kc, j=j, start=start, stop=stop: h.matmul(
                    pv_ps[HH][:, j * 512:(j + 1) * 512],
                    vbuf[par][:, hh, kc, 0:65],
                    probs[HH][kc % 4][:, j * 512:(j + 1) * 512],
                    start=start, stop=stop), "pe")
            pvm[(HH, kc)] = pl.c["pe"]

        for qh in range(2):
            for pair in range(PAIRS):
                par = pair % 2
                act_base = pl.c["act"]
                extra_act = 2 if (qh == 1 and pair >= 2) else 0
                expd = {}
                for kc in range(NKC):
                    expd[("A", kc)] = act_base + extra_act + 2 * kc + 1
                    expd[("B", kc)] = act_base + extra_act + 2 * kc + 2
                W("pe", f"dm_kv{par}", kv_load[("ld", qh * PAIRS + pair)])
                scm = {}
                for kc in range(NKC):
                    if kc >= 4:
                        kcl = kc - 4
                        for hh, HH in ((0, "A"), (1, "B")):
                            if kcl == 0:
                                W("pe", "dve", dtr.get(("ts", 13 if HH == "A" else 15), 0))
                            emit_pv(HH, kcl, par, hh, start=(kcl == 0), stop=False)
                    for hh, HH in ((0, "A"), (1, "B")):
                        h0 = 64 * hh
                        W("pe", "act", expd.get((HH, kc - 1), prevexp[HH]))
                        for j in range(2):
                            DO("pe", lambda h, HH=HH, pair=pair, par=par, h0=h0, kc=kc, qh=qh, j=j: h.matmul(
                                sc_ps[HH][:, j * 512:(j + 1) * 512],
                                kbuf[par][h0:h0 + 64, kc * 128:(kc + 1) * 128],
                                q_fm[h0:h0 + 64, pair, qh * 1024 + j * 512: qh * 1024 + (j + 1) * 512],
                                start=True, stop=True), "pe")
                        scm[(HH, kc)] = pl.c["pe"]
                    if kc <= 3 and pending[0] is not None:
                        for s4 in range(4):
                            defer_step(4 * kc + s4)
                    if kc == 4:
                        if qh == 1 and pair >= 1:
                            if pair == 1:
                                p25_gate[0] = pl.c["dve"]
                            p25_stage(pair - 1)
                for kc in range(NKC):
                    for HH in ("A", "B"):
                        W("act", "pe", scm[(HH, kc)])
                        W("act", "pe", pvm.get((HH, kc - 4), prevpv[(HH, kc % 4)]))
                        got = DO("act", lambda h, HH=HH, kc=kc: h.activation(
                            probs[HH][kc % 4], sc_ps[HH][:], AF.Exp,
                            bias=shift_t[:], scale=ATT_SCALE), "act")
                        assert got == expd[(HH, kc)]
                for kcl in range(NKC - 4, NKC):
                    for hh, HH in ((0, "A"), (1, "B")):
                        W("pe", "act", expd[(HH, kcl)])
                        emit_pv(HH, kcl, par, hh, start=False, stop=(kcl == NKC - 1))
                for HH in ("A", "B"):
                    prevexp[HH] = expd[(HH, NKC - 1)]
                    for k in range(4):
                        prevpv[(HH, k)] = pvm[(HH, NKC - 4 + k)]
                W("dve", "pe", pvm[("A", NKC - 1)])
                W("dve", "pe", pvm[("B", NKC - 1)])
                W("dve", "pe", last_tr_pe[0])
                DO("dve", lambda h: h.tensor_copy(pvsAB[:, 0:1024], PC[0:65, :]), "dve")
                gB = DO("dve", lambda h: h.tensor_copy(pvsAB[:, 1024:2048], PD[0:65, :]), "dve")
                pending[0] = {"qh": qh, "pair": pair, "gate": gB}
                kv_load[("consumed", qh * PAIRS + pair)] = pl.c["pe"]
                if qh * PAIRS + pair + 2 < 2 * PAIRS:
                    load_pair(qh * PAIRS + pair + 2)

        for s in range(16):
            defer_step(s)
        pending[0] = None
        p25_stage(7)
        p25_stage(8)
        p25_stage(9)
        sn2 = pl.barrier()

        if upto < 3:
            return nc, pl, dict(locals())

        sn25 = pl.barrier()

    if upto < 4:
        return nc, pl, dict(locals())

    out_dma = {}
    wbuf_used = {}
    gu_used = {}
    ydone = {}
    wdma = {}
    mshared_done = [0, 0]

    def logit_tile(t, half, gate0=0):
        tl = (t - half * 8) * 128
        W("pe", "dve", max(wbuf_used.get(("lg", t - 1), 0), gate0))
        for c in range(8):
            DO("pe", lambda h, c=c, tl=tl: h.matmul(
                PC[:, 0:8], h_fm[:, c, tl:tl + 128],
                gate_hi[:].rearrange("p (c e) -> p c e", c=8)[:, c, :],
                start=(c == 0), stop=False), "pe")
            DO("pe", lambda h, c=c, tl=tl: h.matmul(
                PC[:, 0:8], h_fm[:, c, tl:tl + 128],
                gate_lo[:].rearrange("p (c e) -> p c e", c=8)[:, c, :],
                start=False, stop=False), "pe")
            DO("pe", lambda h, c=c, tl=tl: h.matmul(
                PC[:, 0:8], lo_fm[:, c, tl:tl + 128],
                gate_hi[:].rearrange("p (c e) -> p c e", c=8)[:, c, :],
                start=False, stop=(c == 7)), "pe")
        mlg = pl.c["pe"]
        wbuf_used[("lgpe", t)] = mlg
        W("dve", "pe", mlg)
        DO("dve", lambda h: h.tensor_copy(lsb[:], PC[:, 0:8]), "dve")
        wbuf_used[("lg", t)] = pl.c["dve"]
        m = pl.c["dve"]; W("dve", "dve", m)
        DO("dve", lambda h: h.reduce_max(tvb[:], lsb[:], AX.X), "dve")
        m = pl.c["dve"]; W("dve", "dve", m)
        DO("dve", lambda h, t=t: h.tensor_scalar(
            oh_all[:, t * E:(t + 1) * E], lsb[:], tvb[:], None, ALU.is_equal), "dve")
        m = pl.c["dve"]; W("act", "dve", m)
        W("act", "act", pl.c["act"])
        DO("act", lambda h, t=t: h.activation(
            score_all[:, t:t + 1], tvb[:], AF.Sigmoid), "act")

    def emit_3a(half, tr_only=False):
        tiles = range(half * 8, half * 8 + 8)
        for t in tiles:
            par = t % 2
            if half == 1 and t == 8:
                W("dve", "dm_x1w", x1_spill.get(("dma", 14), 0))
                W("dve", "dm_x1w1", x1_spill.get(("dma", 15), 0))
            W("sync", "dve", wbuf_used.get(("x1cons", t - 2), 0))
            W("sync", "dm_x1w" if par == 0 else "dm_x1w1", x1_spill.get(("dma", t), 0))
            mld = DO("sync", lambda h, t=t, par=par: h.dma_start(
                [st0, st1][par], x1_dram[t, :, :]), f"dm_x1r{par}", 16)
            W("dve", f"dm_x1r{par}", mld)
            DO("dve", lambda h, par=par: h.bn_stats(bn6[:, 0:6], [st0, st1][par][:, 0:512]), "dve")
            DO("dve", lambda h, par=par: h.bn_stats(bn6[:, 6:12], [st0, st1][par][:, 512:1024]), "dve")
            m = pl.c["dve"]; W("dve", "dve", m)
            DO("dve", lambda h: h.bn_aggr(agg2[:], bn6[:]), "dve")
            m = pl.c["dve"]; W("dve", "dve", m)
            DO("dve", lambda h: h.tensor_scalar(rt1[:], agg2[:, 1:2], 1.0, LN_E1, ALU.mult, ALU.add), "dve")
            m = pl.c["dve"]; W("act", "dve", m)
            W("act", "act", pl.c["act"])
            DO("act", lambda h: h.activation(rt2[:], rt1[:], AF.Ln), "act")
            m = pl.c["act"]; W("act", "act", m)
            DO("act", lambda h: h.activation(rt1[:], rt2[:], AF.Exp, scale=-0.5), "act")
            m = pl.c["act"]; W("dve", "act", m)
            W("dve", "pe", wbuf_used.get(("htr", t - 1), 0))
            DO("dve", lambda h, par=par: h.tensor_scalar(
                scr1[:], [st0, st1][par], agg2[:, 0:1], rt1[:], ALU.subtract, ALU.mult), "dve")
            wbuf_used[("x1cons", t)] = pl.c["dve"]
            m = pl.c["dve"]; W("dve", "dve", m)
            DO("dve", lambda h, t=t: h.tensor_copy(h_tm[:, t, :], scr1[:]), "dve")
            m = pl.c["dve"]; W("dve", "dve", m)
            DO("dve", lambda h, t=t: h.tensor_sub(scr2[:], scr1[:], h_tm[:, t, :]), "dve")
            mlo = pl.c["dve"]
            for c in range(8):
                keys = ("a", "b") if c % 2 == 0 else ("b", "a")
                pts = (PTa, PTb) if c % 2 == 0 else (PTb, PTa)
                W("pe", "dve", mlo)
                W("pe", "dve", ptg[keys[0]])
                W("pe", "dve", ptg[keys[1]])
                DO("pe", lambda h, pts=pts, t=t, c=c: h.transpose(
                    pts[0], scr1[:, c * 128:(c + 1) * 128], id_f[:]), "pe")
                DO("pe", lambda h, pts=pts, c=c: h.transpose(
                    pts[1], scr2[:, c * 128:(c + 1) * 128], id_f[:]), "pe")
                mpe = pl.c["pe"]
                W("dve", "pe", mpe)
                ptg[keys[0]] = DO("dve", lambda h, pts=pts, c=c, t=t, half=half: h.tensor_copy(
                    h_fm[:, c, (t - half * 8) * 128:(t - half * 8 + 1) * 128], pts[0]), "dve")
                ptg[keys[1]] = DO("dve", lambda h, pts=pts, c=c, t=t, half=half: h.tensor_copy(
                    lo_fm[:, c, (t - half * 8) * 128:(t - half * 8 + 1) * 128], pts[1]), "dve")
            wbuf_used[("htr", t)] = pl.c["pe"]
            mtr = pl.c["dve"]
            if not tr_only:
                logit_tile(t, half, gate0=mtr)

    def emit_shared(half, pre_memset=0, ins=None, ins_down=None, predown_fn=None,
                    act_gate_fn=None):
        e = "sh"
        mh_entry = pl.c["dve"]
        if pre_memset:
            W("dve", "dm_x1w", pre_memset)
        DO("dve", lambda h: h.memset(RQ[:].bitcast(F32), 0.0), "dve")
        W("pe", "dve", mh_entry)
        W("gpsimd", "pe", wdma.get(("shwd_cons",), 0))
        wdma[("wd", e)] = DO("gpsimd", lambda h: h.dma_start(
            wd_v, shd_d.ap().rearrange("(i p) m -> p i m", p=128)), "dm_wd", 16)
        for i in range(16):
            bufp = i % 4
            W("gpsimd", "pe", gu_used.get(("u", e, i - 4), 0))
            wdma[("g", e, i)] = DO("gpsimd", lambda h, i=i, bufp=bufp: h.dma_start(
                wg_buf[bufp], shg_d.ap().rearrange(
                    "(c p) m -> p c m", p=128)[:, :, i * 128:(i + 1) * 128]), f"dm_wg{bufp}", 16)
            wdma[("u", e, i)] = DO("gpsimd", lambda h, i=i, bufp=bufp: h.dma_start(
                wu_buf[bufp], shu_d.ap().rearrange(
                    "(c p) m -> p c m", p=128)[:, :, i * 128:(i + 1) * 128]), f"dm_wg{bufp}", 16)
            W("pe", f"dm_wg{bufp}", wdma[("u", e, i)])
            W("pe", "act", gu_used.get(("prev_silu", 0), 0))
            for c in range(8):
                for j in range(2):
                    DO("pe", lambda h, c=c, j=j, bufp=bufp: h.matmul(
                        PA[:, j * 512:(j + 1) * 512], wg_buf[bufp][:, c, :],
                        h_fm[:, c, j * 512:(j + 1) * 512],
                        start=(c == 0), stop=(c == 7)), "pe")
            mg = pl.c["pe"]
            gu_used[("g", e, i)] = mg
            W("pe", "dve", gu_used.get(("prev_mult", 0), 0))
            for c in range(8):
                for j in range(2):
                    DO("pe", lambda h, c=c, j=j, bufp=bufp: h.matmul(
                        PB[:, j * 512:(j + 1) * 512], wu_buf[bufp][:, c, :],
                        h_fm[:, c, j * 512:(j + 1) * 512],
                        start=(c == 0), stop=(c == 7)), "pe")
            mu = pl.c["pe"]
            gu_used[("u", e, i)] = mu
            W("act", "pe", mg)
            W("act", "act", gu_used.get(("prev_silu", 0), 0))
            if act_gate_fn is not None:
                W("act", "dve", act_gate_fn(i))
            sl = DO("act", lambda h, i=i: h.activation(
                act_v[:, i, :], PA[:], AF.Silu), "act")
            gu_used[("prev_silu", 0)] = sl
            W("dve", "pe", mu)
            W("dve", "act", sl)
            W("dve", "dve", gu_used.get(("prev_mult", 0), 0))
            mm_ = DO("dve", lambda h, i=i: h.tensor_tensor(
                act_v[:, i, :], PB[:], act_v[:, i, :], ALU.mult), "dve")
            gu_used[("prev_mult", 0)] = mm_
            gu_used[("mult", e, i)] = mm_
            if ins is not None:
                ins(i)
        W("pe", "dm_wd", wdma[("wd", e)])
        W("pe", "dve", gu_used[("mult", e, 15)])
        if predown_fn is not None:
            W("pe", "dve", predown_fn())
        for mo in range(8):
            W("pe", "dve", ydone.get((e, mo - 1), 0))
            for i in range(16):
                for j in range(2):
                    DO("pe", lambda h, mo=mo, i=i, j=j: h.matmul(
                        PC[:, j * 512:(j + 1) * 512], wd_v[:, i, mo * 128:(mo + 1) * 128],
                        act_v[:, i, j * 512:(j + 1) * 512],
                        start=(i == 0), stop=(i == 15)), "pe")
            mdown = pl.c["pe"]
            W("dve", "pe", mdown)
            ydone[(e, mo)] = DO("dve", lambda h, mo=mo: h.tensor_tensor(
                F_v[:, mo, :], PC[:], F_v[:, mo, :], ALU.add), "dve")
            if ins_down is not None:
                ins_down(mo)
        wdma[("shwd_cons",)] = pl.c["pe"]
        return ydone[(e, 7)]

    if p3_only:
        x1_spill = {}
    have_p25 = not p3_only
    sc25h = [RW[:, 12288:14336].bitcast(F32), p25b]
    agh = [aggA, agg2b]
    rth = [rtA, rt1b]
    acts25h = {}
    dve25h = {}

    def p25h1_front_dve(t):
        par = t % 2
        W("sync", "dve", x1_spill.get(("cons", t - 2), 0))
        W("sync", "pe", wbuf_used.get(("lgpe", 7), 0))
        mld = DO("sync", lambda h, t=t, par=par: h.dma_start(
            p25st[par], x_d[t * 128:(t + 1) * 128, :]), f"dm_x{par}", 16)
        W("dve", f"dm_x{par}", mld)
        W("dve", "dm_x1w" if par == 0 else "dm_x1w1", x1_spill.get(("dma", t - 2), 0))
        DO("dve", lambda h, t=t, par=par: h.tensor_scalar(
            sc25h[par][:], p25st[par], stats_all[:, 2 * t:2 * t + 1],
            r_all[:, t:t + 1], ALU.subtract, ALU.mult), "dve")
        m = pl.c["dve"]; W("dve", "dve", m)
        DO("dve", lambda h, par=par: h.tensor_mul(sc25h[par][:], sc25h[par][:], sb_s[:]), "dve")
        m = pl.c["dve"]; W("dve", "dve", m)
        DO("dve", lambda h, par=par: h.tensor_add(sc25h[par][:], sc25h[par][:], sb_b[:]), "dve")
        m = pl.c["dve"]; W("dve", "dve", m)
        DO("dve", lambda h, t=t, par=par: h.tensor_add(
            sc25h[par][:], sc25h[par][:], attn_tm[:, t, :]), "dve")
        m = pl.c["dve"]; W("dve", "dve", m)
        DO("dve", lambda h, par=par: h.bn_stats(bn6[:, 0:6], sc25h[par][:, 0:512]), "dve")
        DO("dve", lambda h, par=par: h.bn_stats(bn6[:, 6:12], sc25h[par][:, 512:1024]), "dve")
        m = pl.c["dve"]; W("dve", "dve", m)
        DO("dve", lambda h, par=par: h.bn_aggr(agh[par][:], bn6[:]), "dve")
        m = pl.c["dve"]; W("dve", "dve", m)
        DO("dve", lambda h, par=par: h.tensor_scalar(
            rth[par][:], agh[par][:, 1:2], 1.0, LN_E1, ALU.mult, ALU.add), "dve")
        return pl.c["dve"]

    def p25h1_front_act(t, dcnt):
        par = t % 2
        W("act", "dve", dcnt)
        W("act", "act", pl.c["act"])
        DO("act", lambda h, par=par: h.activation(rt2[:], rth[par][:], AF.Ln), "act")
        m = pl.c["act"]; W("act", "act", m)
        DO("act", lambda h, par=par: h.activation(
            p25r[:, par:par + 1], rt2[:], AF.Exp, scale=-0.5), "act")
        return pl.c["act"]

    def p25h1_back(t, acnt):
        par = t % 2
        W("dve", "act", acnt)
        DO("dve", lambda h, par=par: h.tensor_scalar(
            sc25h[par][:], sc25h[par][:], agh[par][:, 0:1],
            p25r[:, par:par + 1], ALU.subtract, ALU.mult), "dve")
        m = pl.c["dve"]; W("dve", "dve", m)
        DO("dve", lambda h, par=par: h.tensor_add(sc25h[par][:], sc25h[par][:], p25st[par]), "dve")
        x1_spill[("cons", t)] = pl.c["dve"]
        m = pl.c["dve"]
        W("sync", "dve", m)
        x1_spill[("dma", t)] = DO("sync", lambda h, t=t, par=par: h.dma_start(
            x1_dram[t, :, :], sc25h[par][:]), "dm_x1w" if par == 0 else "dm_x1w1", 16)

    def p25h1_stage(p):
        if 0 <= p - 2 <= 7:
            p25h1_back(8 + p - 2, acts25h[8 + p - 2])
        if 0 <= p - 1 <= 7:
            acts25h[8 + p - 1] = p25h1_front_act(8 + p - 1, dve25h[8 + p - 1])
        if p <= 7:
            dve25h[8 + p] = p25h1_front_dve(8 + p)

    pl.barrier()
    emit_3a(0)

    def ins_sh0(i):
        if have_p25 and i <= 9:
            p25h1_stage(i)

    mF = emit_shared(0, ins=ins_sh0,
                     act_gate_fn=lambda i: dve25h.get(i, 0))
    W("sync", "dve", mF)
    mFsp = DO("sync", lambda h: h.dma_start(F_dram[0], RQ[:].bitcast(F32)), "dm_x1w", 16)
    emit_3a(1)
    moh = pl.c["dve"]
    msig = pl.c["act"]
    spst = {}
    scr2bf2 = [scr2.bitcast(BF16)[:, 0:1024], scr2.bitcast(BF16)[:, 1024:2048]]
    mscan_b = [0]
    pfx_fm8b = RW[:, 16384:20480].bitcast(F32)[0:8, :]

    def idx_oh(t):
        key = "a" if t % 2 == 0 else "b"
        ptt = PTa if t % 2 == 0 else PTb
        W("pe", "dve", max(ptg[key], moh))
        DO("pe", lambda h, ptt=ptt, t=t: h.transpose(
            ptt[0:8, 0:128], oh_all[:, t * E:(t + 1) * E], id_f[:]), "pe")
        mpe = pl.c["pe"]
        W("dve", "pe", mpe)
        ptg[key] = DO("dve", lambda h, ptt=ptt, t=t: h.tensor_copy(
            oh_fm8[:, t * 128:(t + 1) * 128], ptt[0:8, 0:128]), "dve")

    def idx_scan():
        m = pl.c["dve"]; W("dve", "dve", m)
        DO("dve", lambda h: h.tensor_tensor_scan(
            pfx_fm8b[:], oh_fm8[:], oh_fm8[:], 0.0, ALU.add, ALU.bypass), "dve")
        mscan_b[0] = pl.c["dve"]

    def idx_slot(t):
        key = "a" if t % 2 == 0 else "b"
        ptt = PTa if t % 2 == 0 else PTb
        W("pe", "dve", max(ptg[key], mscan_b[0]))
        DO("pe", lambda h, ptt=ptt, t=t: h.transpose(
            ptt[:, 0:8], pfx_fm8b[:, t * 128:(t + 1) * 128], id_f[0:8, 0:8]), "pe")
        mpe = pl.c["pe"]
        W("dve", "pe", mpe)
        W("dve", "dve", pl.c["dve"])
        DO("dve", lambda h, ptt=ptt: h.tensor_copy(pfx8[:], ptt[:, 0:8]), "dve")
        ptg[key] = pl.c["dve"]
        m = pl.c["dve"]; W("dve", "dve", m)
        DO("dve", lambda h, t=t: h.tensor_tensor(
            tmp8[:], pfx8[:], oh_all[:, t * E:(t + 1) * E], ALU.mult), "dve")
        m = pl.c["dve"]; W("dve", "dve", m)
        DO("dve", lambda h: h.reduce_sum(slotf[:], tmp8[:], AX.X), "dve")
        m = pl.c["dve"]; W("dve", "dve", m)
        DO("dve", lambda h, t=t: h.tensor_tensor(
            tmp8[:], ebase_b[:], oh_all[:, t * E:(t + 1) * E], ALU.mult), "dve")
        m = pl.c["dve"]; W("dve", "dve", m)
        DO("dve", lambda h: h.reduce_sum(rden[:], tmp8[:], AX.X), "dve")
        m = pl.c["dve"]; W("dve", "dve", m)
        DO("dve", lambda h: h.tensor_add(slotf[:], slotf[:], rden[:]), "dve")
        m = pl.c["dve"]; W("dve", "dve", m)
        DO("dve", lambda h: h.tensor_scalar(
            slotf[:], slotf[:], -1.0, float(E * CAP - 1), ALU.add, ALU.min), "dve")
        m = pl.c["dve"]; W("dve", "dve", m)
        DO("dve", lambda h, t=t: h.tensor_copy(
            idx_all[:, t:t + 1], slotf[:]), "dve")

    def xs_scatter(t):
        sp2 = t % 2
        W("dve", "act", msig)
        W("dve", f"dm_sp{sp2}", spst.get(("dma", t - 2), 0))
        DO("dve", lambda h, t=t, sp2=sp2: h.tensor_scalar(
            scr2bf2[sp2][:], h_tm[:, t, :], score_all[:, t:t + 1], None, ALU.mult), "dve")
        mxs = pl.c["dve"]
        W("gpsimd", "dve", mxs)
        spst[("dma", t)] = DO("gpsimd", lambda h, t=t, sp2=sp2: h.indirect_dma_start(
            xs_cp.ap(), IndirectOffsetOnAxis(ap=idx_all[:, t:t + 1], axis=0),
            scr2bf2[sp2][:], None), f"dm_sp{sp2}", 16)

    def ins_sh1(i):
        if i < 8:
            idx_oh(2 * i)
            idx_oh(2 * i + 1)
            if i == 7:
                idx_scan()
        else:
            idx_slot(2 * (i - 8))
            idx_slot(2 * (i - 8) + 1)

    def ins_down1(mo):
        xs_scatter(2 * mo)
        xs_scatter(2 * mo + 1)

    mF = emit_shared(1, pre_memset=mFsp, ins=ins_sh1, ins_down=ins_down1)
    msp = {0: pl.c["dm_sp0"], 1: pl.c["dm_sp1"]}
    xtr_done = {}
    for e in range(E):
        for rnd in range(4):
            slot = rnd % 2
            W("gpsimd", "dm_sp0", msp[0])
            W("gpsimd", "dm_sp1", msp[1])
            if e == 0 and rnd < 2:
                W("gpsimd", "dm_x1w" if rnd == 0 else "dm_x1w1",
                  x1_spill.get(("dma", 14 + rnd), 0))
            W("gpsimd", "pe", xtr_done.get((e, rnd - 2), xtr_done.get((e - 1, rnd + 2), 0)))
            mxe = DO("gpsimd", lambda h, e=e, rnd=rnd, slot=slot: h.dma_start(
                xs_tm_e[:, slot, :],
                xs_cp[e * CAP + rnd * 128:e * CAP + (rnd + 1) * 128, :]), "dm_xe", 16)
            for c in range(8):
                key = "a" if c % 2 == 0 else "b"
                ptt = PTa if c % 2 == 0 else PTb
                W("pe", "dve", ptg[key])
                if c == 0:
                    W("pe", "dm_xe", mxe)
                DO("pe", lambda h, ptt=ptt, slot=slot, c=c: h.transpose(
                    ptt, xs_tm_e[:, slot, c * 128:(c + 1) * 128], id_f[:]), "pe")
                mpe = pl.c["pe"]
                W("dve", "pe", mpe)
                ptg[key] = DO("dve", lambda h, ptt=ptt, c=c, rnd=rnd: h.tensor_copy(
                    xs_fm_e[:, c, rnd * 128:(rnd + 1) * 128], ptt), "dve")
            xtr_done[(e, rnd)] = pl.c["pe"]
        mxtr = pl.c["dve"]
        W("pe", "dve", mxtr)
        for i in range(16):
            bufp = i % 4
            W("gpsimd", "pe", gu_used.get(("u", e, i - 4), 0))
            wdma[("g", e, i)] = DO("gpsimd", lambda h, e=e, i=i, bufp=bufp: h.dma_start(
                wg_buf[bufp], rtg_d[e].rearrange(
                    "(c p) m -> p c m", p=128)[:, :, i * 128:(i + 1) * 128]), f"dm_wg{bufp}", 16)
            wdma[("u", e, i)] = DO("gpsimd", lambda h, e=e, i=i, bufp=bufp: h.dma_start(
                wu_buf[bufp], rtu_d[e].rearrange(
                    "(c p) m -> p c m", p=128)[:, :, i * 128:(i + 1) * 128]), f"dm_wg{bufp}", 16)
            W("pe", f"dm_wg{bufp}", wdma[("u", e, i)])
            W("pe", "act", gu_used.get(("prev_silu", 0), 0))
            for c in range(8):
                DO("pe", lambda h, c=c, bufp=bufp: h.matmul(
                    PA[:, 0:512], wg_buf[bufp][:, c, :],
                    xs_fm_e[:, c, :],
                    start=(c == 0), stop=(c == 7)), "pe")
            mg = pl.c["pe"]
            gu_used[("g", e, i)] = mg
            W("pe", "dve", gu_used.get(("prev_mult", 0), 0))
            for c in range(8):
                DO("pe", lambda h, c=c, bufp=bufp: h.matmul(
                    PB[:, 0:512], wu_buf[bufp][:, c, :],
                    xs_fm_e[:, c, :],
                    start=(c == 0), stop=(c == 7)), "pe")
            mu = pl.c["pe"]
            gu_used[("u", e, i)] = mu
            W("act", "pe", mg)
            W("act", "pe", ydone.get(("wd_cons", e - 1), wdma.get(("shwd_cons",), 0)))
            W("act", "act", gu_used.get(("prev_silu", 0), 0))
            sl = DO("act", lambda h, i=i: h.activation(
                act_sp_v[:, i, :], PA[:, 0:512], AF.Silu), "act")
            gu_used[("prev_silu", 0)] = sl
            gu_used[("silu", e, i)] = sl
            W("dve", "pe", mu)
            W("dve", "act", sl)
            W("dve", "dve", gu_used.get(("prev_mult", 0), 0))
            mm_ = DO("dve", lambda h, i=i: h.tensor_tensor(
                act_sp_v[:, i, :], PB[:, 0:512], act_sp_v[:, i, :], ALU.mult), "dve")
            gu_used[("mult", e, i)] = mm_
            gu_used[("prev_mult", 0)] = mm_
        W("gpsimd", "pe", ydone.get(("wd_cons", e - 1), wdma.get(("shwd_cons",), 0)))
        wdma[("wd", e)] = DO("gpsimd", lambda h, e=e: h.dma_start(
            wd_v, rtd_d[e].rearrange("(i p) m -> p i m", p=128)), "dm_wd", 16)
        W("pe", "dm_wd", wdma[("wd", e)])
        W("pe", "dve", gu_used[("mult", e, 15)])
        for mo in range(8):
            W("pe", "dve", ydone.get((e, mo - 1), 0))
            for i in range(16):
                DO("pe", lambda h, mo=mo, i=i: h.matmul(
                    PC[:, 0:512], wd_v[:, i, mo * 128:(mo + 1) * 128],
                    act_sp_v[:, i, :],
                    start=(i == 0), stop=(i == 15)), "pe")
            mdown = pl.c["pe"]
            W("dve", "pe", mdown)
            W("dve", "dve", pl.c["dve"])
            DO("dve", lambda h: h.tensor_copy(scr1[:, 0:512], PC[:, 0:512]), "dve")
            mev = pl.c["dve"]
            ydone[(e, mo)] = mev
            for q in range(4):
                key = "a" if q % 2 == 0 else "b"
                ptt = PTa if q % 2 == 0 else PTb
                W("pe", "dve", max(ptg[key], mev))
                DO("pe", lambda h, ptt=ptt, q=q: h.transpose(
                    ptt, scr1[:, q * 128:(q + 1) * 128], id_f[:]), "pe")
                mpe = pl.c["pe"]
                W("dve", "pe", mpe)
                if mo == 0 and q == 0:
                    W("dve", "dm_ye", ydone.get(("ydma", e - 1), 0))
                ptg[key] = DO("dve", lambda h, ptt=ptt, q=q, mo=mo: h.tensor_copy(
                    y_tm_e[:, q, mo * 128:(mo + 1) * 128], ptt), "dve")
        m = pl.c["dve"]
        W("gpsimd", "dve", m)
        ydone[("ydma", e)] = DO("gpsimd", lambda h, e=e: h.dma_start(
            y_cp[e * CAP:(e + 1) * CAP, :].rearrange("(q p) m -> p q m", p=128),
            y_tm_e[:]), "dm_ye", 16)
        ydone[("wd_cons", e)] = pl.c["pe"]
    mydma_all = pl.c["dm_ye"]
    pl.barrier()
    scrEp = [scr1, xs_tm_e[:, 0, :]]
    ygb = [scr2bf2[0], scr2bf2[1], RW[:, 14336:15360], RW[:, 15360:16384]]
    ygs = {}
    mlds = {}

    def ep_gather(t):
        sp4 = t % 4
        W("gpsimd", "dm_ye", mydma_all)
        W("gpsimd", "dve", out_dma.get(("ycons", t - 4), 0))
        ygs[t] = DO("gpsimd", lambda h, t=t, sp4=sp4: h.indirect_dma_start(
            ygb[sp4][:], None,
            y_cp.ap(), IndirectOffsetOnAxis(ap=idx_all[:, t:t + 1], axis=0)),
            f"dm_yg{sp4}", 16)

    def ep_segA(t):
        par = t % 2
        half = t // 8
        W("sync", "dve", out_dma.get(("x1cons", t - 2), 0))
        mlds[t] = DO("sync", lambda h, t=t, par=par: h.dma_start(
            [st0, st1][par], x1_dram[t, :, :]), f"dm_x1r{par}", 16)
        mldF = 0
        if half == 0:
            W("sync", "pe", out_dma.get(("fcons", t - 2), 0))
            mldF = DO("sync", lambda h, t=t, par=par: h.dma_start(
                scrF[par], F_dram[0].rearrange("p (c n) -> p c n", c=8)[:, :, t * 128:(t + 1) * 128]),
                "dm_x1w" if par == 0 else "dm_x1w1", 16)
        W("dve", "dve", out_dma.get(("x1cons", t - 2), 0))
        W("dve", "out", out_dma.get(("o", t - 2), 0))
        for c in range(8):
            tps = PA if c % 2 == 0 else PB
            W("pe", "dve", g_ep[c % 2])
            if half == 0:
                W("pe", "dm_x1w" if par == 0 else "dm_x1w1", mldF)
            DO("pe", lambda h, tps=tps, c=c, t=t, half=half, par=par: h.transpose(
                tps[:, 0:128],
                (scrF[par][:, c, :] if half == 0
                 else F_v[:, c, (t - 8) * 128:(t - 7) * 128]), id_f[:]), "pe")
            mpe = pl.c["pe"]
            W("dve", "pe", mpe)
            g_ep[c % 2] = DO("dve", lambda h, tps=tps, c=c, par=par: h.tensor_copy(
                scrEp[par][:, c * 128:(c + 1) * 128], tps[:, 0:128]), "dve")
        out_dma[("fcons", t)] = pl.c["pe"]
        if t + 2 < NT:
            ep_gather(t + 2)
        sp4 = t % 4
        m = pl.c["dve"]; W("dve", "dve", m)
        W("dve", f"dm_yg{sp4}", ygs[t])
        DO("dve", lambda h, sp4=sp4, par=par: h.tensor_add(
            scrEp[par][:], scrEp[par][:], ygb[sp4][:]), "dve")
        out_dma[("ycons", t)] = pl.c["dve"]

    def ep_segB(t):
        par = t % 2
        m = pl.c["dve"]; W("dve", "dve", m)
        DO("dve", lambda h, par=par: h.bn_stats(bn6[:, 0:6], scrEp[par][:, 0:512]), "dve")
        DO("dve", lambda h, par=par: h.bn_stats(bn6[:, 6:12], scrEp[par][:, 512:1024]), "dve")
        m = pl.c["dve"]; W("dve", "dve", m)
        DO("dve", lambda h: h.bn_aggr(agg2[:], bn6[:]), "dve")
        m = pl.c["dve"]; W("dve", "dve", m)
        DO("dve", lambda h: h.tensor_scalar(rt1[:], agg2[:, 1:2], 1.0, LN_E1, ALU.mult, ALU.add), "dve")
        m = pl.c["dve"]; W("act", "dve", m)
        W("act", "act", pl.c["act"])
        DO("act", lambda h: h.activation(rt2[:], rt1[:], AF.Ln), "act")
        m = pl.c["act"]; W("act", "act", m)
        DO("act", lambda h: h.activation(rt1[:], rt2[:], AF.Exp, scale=-0.5), "act")
        m = pl.c["act"]; W("dve", "act", m)
        DO("dve", lambda h, par=par: h.tensor_scalar(
            scrEp[par][:], scrEp[par][:], agg2[:, 0:1], rt1[:], ALU.subtract, ALU.mult), "dve")
        m = pl.c["dve"]; W("dve", "dve", m)
        W("dve", f"dm_x1r{par}", mlds[t])
        DO("dve", lambda h, par=par: h.tensor_add(
            scrEp[par][:], scrEp[par][:], [st0, st1][par]), "dve")
        out_dma[("x1cons", t)] = pl.c["dve"]
        m = pl.c["dve"]
        W("sync", "dve", m)
        out_dma[("o", t)] = DO("sync", lambda h, t=t, par=par: h.dma_start(
            out_d[t * 128:(t + 1) * 128, :], scrEp[par][:]), "out", 16)

    ep_gather(0)
    ep_gather(1)
    for t in range(NT + 1):
        if t < NT:
            ep_segA(t)
        if t >= 1:
            ep_segB(t - 1)

    W("sync", "out", pl.c["out"])

    return nc, pl, dict(locals())


def emit(nc, pl):
    with nc.Block() as block:
        @block.tensor
        def _(t):
            for f in pl.q["pe"]:
                f(t)

        @block.scalar
        def _(sc):
            for f in pl.q["act"]:
                f(sc)

        @block.vector
        def _(v):
            for f in pl.q["dve"]:
                f(v)

        @block.gpsimd
        def _(g):
            for f in pl.q["gpsimd"]:
                f(g)

        @block.sync
        def _(s):
            for f in pl.q["sync"]:
                f(s)
    return nc


_IDB = np.eye(128, dtype=ml_dtypes.bfloat16)
_IDF = np.eye(128, dtype=np.float32)


def make_in_maps(inputs):
    x = np.asarray(inputs["L0_attn_x"], dtype=np.float32)
    common = {
        "ln_scale": np.ascontiguousarray(inputs["L0_attn_ln_scale"], dtype=np.float32),
        "ln_bias": np.ascontiguousarray(inputs["L0_attn_ln_bias"], dtype=np.float32),
        "wq": np.ascontiguousarray(inputs["L0_attn_wq"], dtype=np.float32),
        "wk": np.ascontiguousarray(inputs["L0_attn_wk"], dtype=np.float32),
        "wv": np.ascontiguousarray(inputs["L0_attn_wv"], dtype=np.float32),
        "gate_w": np.ascontiguousarray(inputs["L0_ffn_gate_w"], dtype=np.float32),
        "sh_gate": np.ascontiguousarray(inputs["L0_ffn_sh_gate"], dtype=np.float32),
        "sh_up": np.ascontiguousarray(inputs["L0_ffn_sh_up"], dtype=np.float32),
        "sh_down": np.ascontiguousarray(inputs["L0_ffn_sh_down"], dtype=np.float32),
        "rt_gate": np.ascontiguousarray(inputs["L0_ffn_rt_gate"], dtype=np.float32),
        "rt_up": np.ascontiguousarray(inputs["L0_ffn_rt_up"], dtype=np.float32),
        "rt_down": np.ascontiguousarray(inputs["L0_ffn_rt_down"], dtype=np.float32),
        "ident_bf": _IDB,
        "ident_f32": _IDF,
        "iota128": np.arange(128, dtype=np.float32).reshape(128, 1),
        "ebase": (np.arange(E) * CAP).astype(np.float32),
    }
    in_maps = []
    for c in range(NC_N):
        b, tb = c // 4, c % 4
        xb = x[b]
        xr = np.ascontiguousarray(np.concatenate([xb[tb * TC:], xb[:tb * TC]], axis=0))
        m = dict(common)
        m["x"] = xr
        in_maps.append(m)
    return in_maps


def kernel(**inputs):
    nc, pl, _h = build_program()
    emit(nc, pl)
    in_maps = make_in_maps(inputs)
    res = run_bass_kernel_spmd(nc, in_maps, core_ids=list(range(NC_N)))
    full = np.empty((B, S, D), np.float32)
    for c in range(NC_N):
        b, tb = c // 4, c % 4
        full[b, tb * TC:(tb + 1) * TC] = np.asarray(res.results[c]["out"])
    return full
